# version 13
# baseline (speedup 1.0000x reference)
import math
from contextlib import ExitStack

import numpy as np

import concourse.bass as bass
import concourse.mybir as mybir
from concourse.bass_utils import run_bass_kernel_spmd

F32 = mybir.dt.float32
BF16 = mybir.dt.bfloat16
I32 = mybir.dt.int32
AF = mybir.ActivationFunctionType
ALU = mybir.AluOpType

T = 4096
D = 1024
NST = 16
TC = 256
NCH = T // TC
NJ = D // 128
SEG = TC + 2
SCW = NST * SEG
DFF = 4096
NFB = DFF // 128
EPS = 1e-5
TWO_PI_SAFE = 6.2831845


class Sched:

    ENGINES = ("sync", "vector", "scalar", "tensor", "gpsimd")

    def __init__(self, nc, es):
        self.nc = nc
        self.ops = {e: [] for e in self.ENGINES}
        self.sem_h = {}
        self.cnt = {}
        for name in ("ld", "lg", "V", "A", "Tn"):
            self.sem_h[name] = es.enter_context(nc.semaphore(name))
            self.cnt[name] = 0

    def emit(self, eng, fn, waits=(), inc=None, amt=1):
        wl = {}
        for s in waits:
            name, val = s
            if val > 0:
                wl[name] = max(wl.get(name, 0), val)
        self.ops[eng].append((sorted(wl.items()), fn, inc, amt))
        if inc is not None:
            self.cnt[inc] += amt
        return self.cnt[inc] if inc is not None else None

    def c(self, sem):
        return self.cnt[sem]

    def replay(self, ename, eng):
        need_drain = ename in ("vector", "scalar")
        first = True
        for waits, fn, inc, amt in self.ops[ename]:
            if need_drain and not first:
                eng.drain()
            first = False
            for s, v in waits:
                eng.wait_ge(self.sem_h[s], v)
            inst = fn(eng)
            if inc is not None:
                if inst is not None:
                    inst.then_inc(self.sem_h[inc], amt)
                else:
                    eng.sem_inc(self.sem_h[inc], amt)


def build_nc():
    nc = bass.Bass("TRN2", target_bir_lowering=False,
                   detect_race_conditions=False)

    def din(name, shape, dt=F32):
        return nc.dram_tensor(name, shape, dt, kind="ExternalInput").ap()

    xT = din("xT", [D, T])
    wd = din("wd", [D, D])
    w1 = din("w1", [D, DFF])
    w2 = din("w2", [DFF, D])
    wbp = din("wbp", [128, 128])
    wcp = din("wcp", [128, 128])
    alogp = din("alogp", [128, 128])
    gcol = din("gcol", [128, NJ])
    bcol = din("bcol", [128, NJ])
    bdcol = din("bdcol", [128, NJ])
    dskcol = din("dskcol", [128, NJ])
    b2col = din("b2col", [128, NJ])
    b1col = din("b1col", [128, NFB])
    qpcol = din("qpcol", [128, NFB])
    wrcol = din("wrcol", [128, NFB])
    wicol = din("wicol", [128, NFB])
    bcbiasB = din("bcbiasB", [16, 1])
    bcbiasC = din("bcbiasC", [16, 1])
    selcat = din("selcat", [16, 16 * 128])
    eye_in = din("eye_in", [128, 128])
    onescol_in = din("onescol_in", [128, 1])
    onesrow = din("onesrow", [1, 128])

    outT = nc.dram_tensor("outT", [D, T], F32, kind="ExternalOutput").ap()
    w1X = nc.dram_tensor("w1X", [128, NFB * 1024], BF16, kind="Internal").ap()
    w2X = nc.dram_tensor("w2X", [128, NFB * 1024], BF16, kind="Internal").ap()

    es = ExitStack()
    with es:
        def sb(name, shape, dt=F32):
            return es.enter_context(nc.sbuf_tensor(name, shape, dt))

        xt = sb("xt", [128, 2 * NJ * TC])
        out_t = sb("out_t", [128, 2 * NJ * TC])
        xn = sb("xn", [128, NJ * TC])
        xnbf = sb("xnbf", [128, NJ * TC], BF16)
        xn2 = sb("xn2", [128, NJ * TC])
        xn2bf = sb("xn2bf", [128, NJ * TC], BF16)
        delta = sb("delta", [128, NJ * TC])
        ubf = sb("ubf", [128, NJ * TC], BF16)
        dAb = sb("dAb", [128, SCW])
        wb = sb("wb", [128, SCW], BF16)
        Hb = sb("Hb", [128, SCW], BF16)
        zb = sb("zb", [128, NST * TC], BF16)
        Bexp = sb("Bexp", [128, NST * TC], BF16)
        Cexp = sb("Cexp", [128, NST * TC], BF16)
        stats = sb("stats", [1, 1024])
        hstate = sb("hstate", [128, 128], BF16)
        A_pack = sb("A_pack", [128, 128])
        WBbf = sb("WBbf", [128, 128], BF16)
        WCbf = sb("WCbf", [128, 128], BF16)
        Wdbf = sb("Wdbf", [128, NJ * 1024], BF16)
        w1buf = sb("w1buf", [128, 2 * 1024], BF16)
        w2buf = sb("w2buf", [128, 2 * 4096], BF16)
        y_sb = sb("y_sb", [128, NJ * TC])
        act = sb("act", [128, NFB * TC], BF16)
        fs_s = sb("fs_s", [128, 2 * TC])
        fs_c = sb("fs_c", [128, 2 * TC])
        eye_bf = sb("eye_bf", [128, 128], BF16)
        onescol_bf = sb("onescol_bf", [128, 1], BF16)
        selcat_bf = sb("selcat_bf", [16, 16 * 128], BF16)
        bcmB = sb("bcmB", [16, TC], BF16)
        bcmC = sb("bcmC", [16, TC], BF16)
        gcol_s = sb("gcol_s", [128, NJ])
        bcol_s = sb("bcol_s", [128, NJ])
        bdcol_s = sb("bdcol_s", [128, NJ])
        dskcol_s = sb("dskcol_s", [128, NJ])
        b2col_s = sb("b2col_s", [128, NJ])
        b1col_s = sb("b1col_s", [128, NFB])
        vec_s = sb("vec_s", [128, NFB])
        vec_c = sb("vec_c", [128, NFB])
        P1 = sb("P1", [128, NFB])
        P2 = sb("P2", [128, NFB])
        qp_s = sb("qp_s", [128, NFB])
        wr_s = sb("wr_s", [128, NFB])
        wi_s = sb("wi_s", [128, NFB])
        scr1 = sb("scr1", [128, NFB])
        scr2 = sb("scr2", [128, NFB])
        bcbB_s = sb("bcbB_s", [16, 1])
        bcbC_s = sb("bcbC_s", [16, 1])
        onesrow_s = sb("onesrow_s", [1, 128])
        cst = sb("cst", [128, 4])
        thr_s = sb("thr_s", [128, 16])

        def ps(name):
            return es.enter_context(nc.psum_tensor(name, [128, 512], F32))
        psA = ps("psA")
        psB = ps("psB")
        psC = ps("psC")
        psE = ps("psE")
        psF = [ps(f"psF{i}") for i in range(4)]

        S = Sched(nc, es)
        block = es.enter_context(nc.Block())

        xT_r = xT.rearrange("(j p) t -> p j t", p=128)
        outT_r = outT.rearrange("(j p) t -> p j t", p=128)

        def dma(out_ap, in_ap):
            def f(e, o=out_ap, i=in_ap):
                return e.dma_start(out=o, in_=i)
            return f

        for o, i in [
            (A_pack[:, :], alogp), (gcol_s[:, :], gcol), (bcol_s[:, :], bcol),
            (bdcol_s[:, :], bdcol), (dskcol_s[:, :], dskcol),
            (b2col_s[:, :], b2col), (b1col_s[:, :], b1col), (qp_s[:, :], qpcol),
            (wr_s[:, :], wrcol), (bcbB_s[:, :], bcbiasB),
            (bcbC_s[:, :], bcbiasC), (wi_s[:, :], wicol),
            (onesrow_s[:, :], onesrow),
        ]:
            S.emit("sync", dma(o, i), inc="ld", amt=16)
        S.emit("sync", dma(dAb[:16, 0:2048], selcat), inc="ld", amt=16)
        S.emit("sync", dma(dAb[:, 2048:2176], eye_in), inc="ld", amt=16)
        S.emit("sync", dma(dAb[:, 2176:2177], onescol_in), inc="ld", amt=16)
        S.emit("sync", dma(dAb[:, 2432:2560], wbp), inc="ld", amt=16)
        S.emit("sync", dma(dAb[:, 2560:2688], wcp), inc="ld", amt=16)
        ld_small = S.c("ld")
        S.emit("vector", lambda e: e.memset(cst[:, 0:1], EPS))
        S.emit("vector", lambda e: e.memset(cst[:, 1:2], 1.0))
        S.emit("vector", lambda e: e.memset(cst[:, 2:3], 0.0))
        S.emit("vector", lambda e: e.memset(cst[:, 3:4], 1.5707962))
        for jj in range(16):
            b = -(jj + 0.5) if jj < 8 else (jj - 8 + 0.5)
            S.emit("vector", (lambda e, jj=jj, b=b: e.memset(
                thr_s[:, jj:jj + 1], b)))
        S.emit("vector", lambda e: e.tensor_copy(selcat_bf[:, :], dAb[:16, 0:2048]),
               waits=[("ld", ld_small)])
        S.emit("vector", lambda e: e.tensor_copy(eye_bf[:, :], dAb[:, 2048:2176]))
        S.emit("vector", lambda e: e.tensor_copy(onescol_bf[:, :], dAb[:, 2176:2177]))
        S.emit("vector", lambda e: e.tensor_copy(WBbf[:, :], dAb[:, 2432:2560]))
        v_consts = S.emit("vector",
                          lambda e: e.tensor_copy(WCbf[:, :], dAb[:, 2560:2688]),
                          inc="V")

        a_exp = S.emit("scalar",
                       lambda e: e.activation(A_pack[:, :], A_pack[:, :], AF.Exp,
                                              bias=0.0, scale=1.0),
                       waits=[("ld", ld_small)], inc="A")
        S.emit("vector", lambda e: e.tensor_scalar_mul(A_pack[:, :], A_pack[:, :],
                                                       -1.0),
               waits=[("A", a_exp)])
        v_y = S.emit("vector",
                     lambda e: e.tensor_scalar_mul(scr1[:, :], qp_s[:, :],
                                                   1.0 / (2 * math.pi)),
                     inc="V")
        for jj in range(16):
            b = -(jj + 0.5) if jj < 8 else (jj - 8 + 0.5)
            S.emit("scalar",
                   (lambda e, jj=jj: e.activation(
                       fs_s[:, jj * NFB:(jj + 1) * NFB], scr1[:, :], AF.Sign,
                       bias=thr_s[:, jj:jj + 1], scale=1.0)),
                   waits=[("V", v_y)] if jj == 0 else [],
                   inc=("A" if jj == 15 else None))
        a_signs = S.c("A")
        S.emit("vector", lambda e: e.tensor_copy(scr2[:, :], fs_s[:, 0:NFB]),
               waits=[("A", a_signs)])
        for jj in range(1, 16):
            S.emit("vector", (lambda e, jj=jj: e.tensor_add(
                scr2[:, :], scr2[:, :], fs_s[:, jj * NFB:(jj + 1) * NFB])))
        S.emit("vector", lambda e: e.tensor_scalar_mul(scr2[:, :], scr2[:, :],
                                                       0.5))
        v_frac = S.emit("vector",
                        lambda e: e.tensor_sub(scr1[:, :], scr1[:, :], scr2[:, :]),
                        inc="V")
        S.emit("scalar", lambda e: e.activation(scr2[:, :], scr1[:, :], AF.Abs,
                                                bias=0.0, scale=1.0),
               waits=[("V", v_frac)])
        S.emit("scalar", lambda e: e.activation(P1[:, :], scr1[:, :], AF.Sin,
                                                bias=0.0, scale=TWO_PI_SAFE))
        a_sc2 = S.emit("scalar",
                       lambda e: e.activation(P2[:, :], scr2[:, :], AF.Sin,
                                              bias=cst[:, 3:4],
                                              scale=-TWO_PI_SAFE),
                       inc="A")
        S.emit("vector", lambda e: e.tensor_mul(scr1[:, :], P2[:, :], wr_s[:, :]),
               waits=[("A", a_sc2)])
        S.emit("vector", lambda e: e.tensor_mul(scr2[:, :], P1[:, :], wi_s[:, :]))
        S.emit("vector", lambda e: e.tensor_add(scr1[:, :], scr1[:, :], scr2[:, :]))
        S.emit("vector", lambda e: e.tensor_mul(scr1[:, :], scr1[:, :], wi_s[:, :]))
        S.emit("vector", lambda e: e.tensor_scalar_mul(scr1[:, :], scr1[:, :], 0.1))
        S.emit("vector", lambda e: e.tensor_mul(P1[:, :], P1[:, :], scr1[:, :]))
        S.emit("vector", lambda e: e.tensor_mul(P2[:, :], P2[:, :], scr1[:, :]))
        S.emit("vector", lambda e: e.tensor_scalar_mul(vec_s[:, :], b1col_s[:, :],
                                                       0.1))
        S.emit("vector", lambda e: e.tensor_scalar(vec_c[:, :], b1col_s[:, :], 0.1,
                                                   math.pi / 2.0, ALU.mult,
                                                   ALU.add),
               inc="V")

        ld_x = [S.emit("sync",
                       dma(xt[:, 0:NJ * TC].rearrange("p (j t) -> p j t", j=NJ),
                           xT_r[:, :, 0:TC]), inc="ld", amt=16)]

        wd_r = wd.rearrange("(k p) d -> p k d", p=128)
        w1_r = w1.rearrange("(k p) f -> p k f", p=128)
        w2_r = w2.rearrange("(fb p) d -> p fb d", p=128)
        w1X_r = w1X.rearrange("p (fb j c) -> p fb j c", fb=NFB, j=NJ)

        stage_f = [dAb[:, 0:2048], dAb[:, 2048:4096]]
        stage_b = [wb[:, 0:2048], wb[:, 2048:4096]]
        v_cast_prev = {0: v_consts, 1: v_consts}
        lg_out_prev = {0: 0, 1: 0}

        for k in range(NJ):
            p = k % 2
            ldv = S.emit("sync", dma(stage_f[p][:, 0:1024], wd_r[:, k, :]),
                         waits=[("V", v_cast_prev[p])], inc="ld", amt=16)
            v_cast_prev[p] = S.emit(
                "vector",
                (lambda e, k=k, p=p: e.tensor_copy(
                    Wdbf[:, k * 1024:(k + 1) * 1024], stage_f[p][:, 0:1024])),
                waits=[("ld", ldv)], inc="V")
        w2X_r = w2X.rearrange("p (j fb c) -> p j fb c", j=NJ, fb=NFB)
        for fb in range(NFB):
            p = fb % 2
            ldv = S.emit("sync", dma(stage_f[p][:, 0:1024], w2_r[:, fb, :]),
                         waits=[("V", v_cast_prev[p])], inc="ld", amt=16)
            v_cast_prev[p] = S.emit(
                "vector",
                (lambda e, p=p: e.tensor_copy(stage_b[p][:, 0:1024],
                                              stage_f[p][:, 0:1024])),
                waits=[("ld", ldv), ("lg", lg_out_prev[p])], inc="V")
            lg_out_prev[p] = S.emit(
                "gpsimd",
                (lambda e, fb=fb, p=p: e.dma_start(
                    out=w2X_r[:, :, fb, :],
                    in_=stage_b[p][:, 0:1024].rearrange(
                        "p (j c) -> p j c", j=NJ))),
                waits=[("V", v_cast_prev[p])], inc="lg", amt=16)
        for k in range(NJ):
            for h in range(2):
                p = h
                ldv = S.emit("sync",
                             dma(stage_f[p][:, 0:2048],
                                 w1_r[:, k, h * 2048:(h + 1) * 2048]),
                             waits=[("V", v_cast_prev[p])], inc="ld", amt=16)
                v_cast_prev[p] = S.emit(
                    "vector",
                    (lambda e, p=p: e.tensor_copy(stage_b[p][:, 0:2048],
                                                  stage_f[p][:, 0:2048])),
                    waits=[("ld", ldv), ("lg", lg_out_prev[p])], inc="V")
                lg_out_prev[p] = S.emit(
                    "gpsimd",
                    (lambda e, k=k, h=h, p=p: e.dma_start(
                        out=w1X_r[:, h * 16:(h + 1) * 16, k, :],
                        in_=stage_b[p][:, 0:2048].rearrange(
                            "p (fb c) -> p fb c", fb=16))),
                    waits=[("V", v_cast_prev[p])], inc="lg", amt=16)
        lg_prologue = S.c("lg")

        S.emit("vector", lambda e: e.memset(dAb[:, :], 0.0),
               waits=[("lg", lg_prologue)])
        S.emit("vector", lambda e: e.memset(wb[:, :], 0.0))
        v_zeroed = S.emit("vector", lambda e: e.memset(hstate[:, :], 0.0),
                          inc="V")

        dA_seg = dAb[:].rearrange("p (n s) -> p n s", n=NST)
        wb_seg = wb[:].rearrange("p (n s) -> p n s", n=NST)
        Hb_seg = Hb[:].rearrange("p (n s) -> p n s", n=NST)
        zb_seg = zb[:].rearrange("p (n t) -> p n t", n=NST)
        Bexp_seg = Bexp[:].rearrange("p (n t) -> p n t", n=NST)
        Cexp_seg = Cexp[:].rearrange("p (n t) -> p n t", n=NST)

        st_mu = stats[:, 0:256]
        st_msq = stats[:, 256:512]
        st_tmp = stats[:, 512:768]
        st_rstd = stats[:, 768:1024]

        v_asm = [0] * (NCH + 1)
        lg_store = [0] * (NCH + 1)
        a_expevac_prev = 0
        t_y_prev_chunk = 0
        v_xn_prev = 0
        v_xn2_prev = 0
        v_bcm_prev = 0
        t_hid_prev = [0, 0]
        t_w2_prev = [0, 0]
        v_ffn_prev = [0, 0]
        v_asm_prevJ = [0, 0]
        a_yevac_prev = [0, 0]
        a_dA_last = 0

        for c in range(NCH):
            pc = c % 2
            xo = pc * NJ * TC
            x_c = xt[:, xo:xo + NJ * TC]
            out_c = out_t[:, xo:xo + NJ * TC]
            xbf = ubf
            sqbf = zb[:, 0:NJ * TC]

            if c + 1 < NCH:
                ld_x.append(S.emit(
                    "sync",
                    (lambda e, c=c: e.dma_start(
                        out=xt[:, ((c + 1) % 2) * NJ * TC:
                               (((c + 1) % 2) + 1) * NJ * TC].rearrange(
                            "p (j t) -> p j t", j=NJ),
                        in_=xT_r[:, :, (c + 1) * TC:(c + 2) * TC])),
                    waits=[("V", v_asm[c - 1] if c >= 1 else 0)],
                    inc="ld", amt=16))

            v_xbf = S.emit("vector",
                           (lambda e, x_c=x_c: e.tensor_copy(xbf[:, :], x_c)),
                           waits=[("ld", ld_x[c]), ("V", v_zeroed)], inc="V")
            a_sq = S.emit("scalar",
                          lambda e: e.activation(sqbf, xbf[:, :], AF.Square,
                                                 bias=0.0, scale=1.0),
                          waits=[("V", v_xbf), ("Tn", t_y_prev_chunk)], inc="A")
            for j in range(NJ):
                S.emit("tensor",
                       (lambda e, j=j: e.matmul(
                           psA[0:1, 0:256], onescol_bf[:, :],
                           xbf[:, j * TC:(j + 1) * TC],
                           start=(j == 0), stop=(j == NJ - 1),
                           skip_group_check=True)),
                       waits=([("V", v_xbf), ("V", v_xn_prev),
                               ("A", a_expevac_prev)] if j == 0 else []))
            t_sums = 0
            for j in range(NJ):
                t_sums = S.emit(
                    "tensor",
                    (lambda e, j=j: e.matmul(
                        psA[0:1, 256:512], onescol_bf[:, :],
                        sqbf[:, j * TC:(j + 1) * TC],
                        start=(j == 0), stop=(j == NJ - 1),
                        skip_group_check=True)),
                    waits=[("A", a_sq)] if j == 0 else [],
                    inc=("Tn" if j == NJ - 1 else None))
            S.emit("vector", lambda e: e.tensor_scalar_mul(st_mu, psA[0:1, 0:256],
                                                           1.0 / D),
                   waits=[("Tn", t_sums)])
            S.emit("vector", lambda e: e.tensor_scalar_mul(st_msq,
                                                           psA[0:1, 256:512],
                                                           1.0 / D))
            S.emit("vector", lambda e: e.tensor_mul(st_tmp, st_mu, st_mu))
            v_var = S.emit("vector",
                           lambda e: e.tensor_sub(st_msq, st_msq, st_tmp),
                           inc="V")
            a_rstd = S.emit("scalar",
                            lambda e: e.activation(st_rstd, st_msq, AF.Sqrt,
                                                   bias=cst[0:1, 0:1], scale=1.0),
                            waits=[("V", v_var)], inc="A")
            S.emit("vector", lambda e: e.reciprocal(st_rstd, st_rstd),
                   waits=[("A", a_rstd)])
            v_beta = S.emit("vector",
                            lambda e: e.scalar_tensor_tensor(
                                st_tmp, st_mu, -1.0, st_rstd,
                                ALU.mult, ALU.mult),
                            inc="V")
            S.emit("tensor", lambda e: e.matmul(psA[:, 256:512], onesrow_s[:, :],
                                                st_rstd, start=True, stop=True,
                                                skip_group_check=True),
                   waits=[("V", v_beta), ("A", a_expevac_prev)])
            t_eab = S.emit("tensor",
                           lambda e: e.matmul(psA[:, 0:256], onesrow_s[:, :],
                                              st_tmp, start=True, stop=True,
                                              skip_group_check=True),
                           inc="Tn")
            for j in range(NJ):
                S.emit("vector",
                       (lambda e, j=j, x_c=x_c: e.tensor_mul(
                           xn[:, j * TC:(j + 1) * TC],
                           x_c[:, j * TC:(j + 1) * TC], psA[:, 256:512])),
                       waits=[("Tn", t_eab)] if j == 0 else [])
                S.emit("vector",
                       (lambda e, j=j: e.tensor_add(
                           xn[:, j * TC:(j + 1) * TC],
                           xn[:, j * TC:(j + 1) * TC], psA[:, 0:256])))
            v_xn = S.emit("vector",
                          lambda e: e.tensor_copy(xnbf[:, :], xn[:, :]), inc="V")

            a_sq2 = S.emit("scalar",
                           lambda e: e.activation(sqbf, xnbf[:, :], AF.Square,
                                                  bias=0.0, scale=1.0),
                           waits=[("V", v_xn)], inc="A")
            t_sum2 = 0
            for j in range(NJ):
                t_sum2 = S.emit(
                    "tensor",
                    (lambda e, j=j: e.matmul(
                        psC[0:1, 0:256], onescol_bf[:, :],
                        sqbf[:, j * TC:(j + 1) * TC],
                        start=(j == 0), stop=(j == NJ - 1),
                        skip_group_check=True)),
                    waits=([("A", a_sq2), ("V", v_xn2_prev),
                            ("V", v_bcm_prev)] if j == 0 else []),
                    inc=("Tn" if j == NJ - 1 else None))
            v_m2 = S.emit("vector",
                          lambda e: e.tensor_scalar_mul(st_mu, psC[0:1, 0:256],
                                                        1.0 / D),
                          waits=[("Tn", t_sum2)], inc="V")
            a_rstd2 = S.emit("scalar",
                             lambda e: e.activation(st_msq, st_mu, AF.Sqrt,
                                                    bias=cst[0:1, 0:1], scale=1.0),
                             waits=[("V", v_m2)], inc="A")
            v_rstd2 = S.emit("vector", lambda e: e.reciprocal(st_msq, st_msq),
                             waits=[("A", a_rstd2)], inc="V")
            t_er2 = S.emit("tensor",
                           lambda e: e.matmul(psC[:, 256:512], onesrow_s[:, :],
                                              st_msq, start=True, stop=True,
                                              skip_group_check=True),
                           waits=[("V", v_rstd2)], inc="Tn")
            for j in range(NJ):
                S.emit("vector",
                       (lambda e, j=j: e.scalar_tensor_tensor(
                           xn2[:, j * TC:(j + 1) * TC],
                           xn[:, j * TC:(j + 1) * TC],
                           gcol_s[:, j:j + 1], psC[:, 256:512],
                           ALU.mult, ALU.mult)),
                       waits=[("Tn", t_er2)] if j == 0 else [])
                S.emit("vector",
                       (lambda e, j=j: e.tensor_scalar_add(
                           xn2[:, j * TC:(j + 1) * TC],
                           xn2[:, j * TC:(j + 1) * TC], bcol_s[:, j:j + 1])))
            v_xn2 = S.emit("vector",
                           lambda e: e.tensor_copy(xn2bf[:, :], xn2[:, :]),
                           inc="V")

            a_sp_J = [0] * NJ
            a_exp_J = [0] * NJ
            for j in range(NJ):
                bank = psB if j % 2 == 0 else psC
                half = ((j // 2) % 2) * 256
                war = [("A", a_exp_J[j - 2])] if j >= 2 else \
                    [("A", a_expevac_prev)]
                t_dj = 0
                for k in range(NJ):
                    t_dj = S.emit(
                        "tensor",
                        (lambda e, j=j, k=k, bank=bank, half=half: e.matmul(
                            bank[:, half:half + 256],
                            Wdbf[:, (k * NJ + j) * 128:(k * NJ + j + 1) * 128],
                            xn2bf[:, k * TC:(k + 1) * TC],
                            start=(k == 0), stop=(k == NJ - 1),
                            skip_group_check=True)),
                        waits=([("V", v_xn2)] + war if k == 0 else []),
                        inc=("Tn" if k == NJ - 1 else None))
                a_exp_J[j] = S.emit(
                    "scalar",
                    (lambda e, j=j, bank=bank, half=half: e.activation(
                        fs_s[:, 0:256], bank[:, half:half + 256], AF.Exp,
                        bias=bdcol_s[:, j:j + 1], scale=1.0)),
                    waits=[("Tn", t_dj)], inc="A")
                a_sp_J[j] = S.emit(
                    "scalar",
                    (lambda e, j=j: e.activation(
                        delta[:, j * TC:(j + 1) * TC], fs_s[:, 0:256], AF.Ln,
                        bias=cst[:, 1:2], scale=1.0)),
                    inc="A")

            t_bc = 0
            for (w_t, cols) in ((WBbf, 0), (WCbf, 256)):
                for k in range(NJ):
                    t_bc = S.emit(
                        "tensor",
                        (lambda e, w_t=w_t, cols=cols, k=k: e.matmul(
                            psC[0:16, cols:cols + 256],
                            w_t[:, k * 16:(k + 1) * 16],
                            xn2bf[:, k * TC:(k + 1) * TC],
                            start=(k == 0), stop=(k == NJ - 1),
                            skip_group_check=True)),
                        waits=([("V", v_xn2), ("A", a_exp_J[NJ - 1])]
                               if k == 0 else []),
                        inc=("Tn" if (k == NJ - 1 and cols == 256) else None))
            S.emit("vector", lambda e: e.tensor_scalar_add(
                bcmB[:, :], psC[0:16, 0:256], bcbB_s[:, :]),
                   waits=[("Tn", t_bc)])
            v_bcm = S.emit("vector",
                           lambda e: e.tensor_scalar_add(
                               bcmC[:, :], psC[0:16, 256:512],
                               bcbC_s[:, :]),
                           inc="V")
            v_bcm_prev = v_bcm

            v_u = S.emit("vector",
                         lambda e: e.tensor_mul(ubf[:, :], delta[:, :],
                                                xn2[:, :]),
                         waits=[("A", a_sp_J[NJ - 1]), ("Tn", t_sums)], inc="V")

            v_wz_prev = v_asm[c - 1] if c >= 1 else 0
            a_piece = []
            for (dst_seg, bcsrc) in ((Bexp_seg, bcmB), (Cexp_seg, bcmC)):
                for n in range(NST):
                    m = len(a_piece)
                    bank = psB if m % 2 == 0 else psC
                    half = ((m // 2) % 2) * 256
                    t_piece = S.emit(
                        "tensor",
                        (lambda e, n=n, bcsrc=bcsrc, bank=bank, half=half:
                         e.matmul(
                            bank[:, half:half + 256],
                            selcat_bf[:, n * 128:(n + 1) * 128],
                            bcsrc[:, :],
                            start=True, stop=True, skip_group_check=True)),
                        waits=(([("V", v_bcm), ("A", a_sp_J[NJ - 1])]
                                if m <= 1 else []) +
                               ([("A", a_piece[m - 2])] if m >= 2 else [])),
                        inc="Tn")
                    a_piece.append(S.emit(
                        "scalar",
                        (lambda e, n=n, dst_seg=dst_seg, bank=bank, half=half:
                         e.activation(dst_seg[:, n, :],
                                      bank[:, half:half + 256],
                                      AF.Copy, bias=0.0, scale=1.0)),
                        waits=[("Tn", t_piece), ("V", v_wz_prev)], inc="A"))
            a_expevac = a_piece[-1]
            a_expevac_prev = a_expevac

            v_scan_prev = 0
            t_y_J = 0
            a_yevac_J = [0] * NJ
            for j in range(NJ):
                for n in range(NST):
                    a_dA_last = S.emit(
                        "scalar",
                        (lambda e, j=j, n=n: e.activation(
                            dA_seg[:, n, 2:SEG], delta[:, j * TC:(j + 1) * TC],
                            AF.Exp, bias=0.0,
                            scale=A_pack[:, j * 16 + n:j * 16 + n + 1])),
                        waits=([("V", v_scan_prev if j > 0 else v_zeroed)]
                               if n == 0 else []),
                        inc=("A" if n == NST - 1 else None))
                S.emit("vector",
                       (lambda e, j=j: e.tensor_copy(
                           wb_seg[:, :, 1:2],
                           hstate[:, j * 16:(j + 1) * 16]
                           .rearrange("p (n o) -> p n o", o=1))),
                       waits=([("A", a_expevac), ("V", v_u)] if j == 0 else []))
                S.emit("vector",
                       (lambda e, j=j: e.tensor_mul(
                           wb_seg[:, :, 2:SEG],
                           ubf[:, j * TC:(j + 1) * TC][:, None, :]
                           .broadcast_to([128, NST, TC]),
                           Bexp_seg[:, :, :])))
                S.emit("vector",
                       (lambda e: e.tensor_tensor_scan(
                           Hb[:, :], dAb[:, :], wb[:, :], cst[:, 2:3],
                           ALU.mult, ALU.add)),
                       waits=[("A", a_dA_last)])
                S.emit("vector",
                       (lambda e, j=j: e.tensor_copy(
                           hstate[:, j * 16:(j + 1) * 16]
                           .rearrange("p (n o) -> p n o", o=1),
                           Hb_seg[:, :, SEG - 1:SEG])))
                v_scan_prev = S.emit(
                    "vector",
                    (lambda e: e.tensor_mul(zb_seg[:, :, :],
                                            Hb_seg[:, :, 2:SEG],
                                            Cexp_seg[:, :, :])),
                    waits=[("Tn", t_y_J if j > 0 else t_sum2)], inc="V")
                ybank = psF[j % 2]
                ywar = a_yevac_J[j - 2] if j >= 2 else a_yevac_prev[j % 2]
                for n in range(NST):
                    t_y_J = S.emit(
                        "tensor",
                        (lambda e, j=j, n=n, ybank=ybank: e.matmul(
                            ybank[:, 0:256],
                            eye_bf[:, :], zb[:, n * TC:(n + 1) * TC],
                            start=(n == 0), stop=(n == NST - 1),
                            skip_group_check=True)),
                        waits=([("V", v_scan_prev), ("A", ywar)]
                               if n == 0 else []),
                        inc=("Tn" if n == NST - 1 else None))
                a_yevac_J[j] = S.emit(
                    "scalar",
                    (lambda e, j=j, ybank=ybank: e.activation(
                        y_sb[:, j * TC:(j + 1) * TC], ybank[:, 0:256],
                        AF.Copy, bias=0.0, scale=1.0)),
                    waits=[("Tn", t_y_J)], inc="A")
            t_y_prev_chunk = t_y_J
            a_yevac_prev = [a_yevac_J[NJ - 2], a_yevac_J[NJ - 1]]

            t_hid_fb = [0] * NFB
            v_ffn_fb = [0] * NFB
            a_relu = 0
            for fb in range(NFB):
                p = fb % 2
                S.emit("sync",
                       (lambda e, fb=fb, p=p: e.dma_start(
                           out=w1buf[:, p * 1024:(p + 1) * 1024],
                           in_=w1X[:, fb * 1024:(fb + 1) * 1024])),
                       waits=[("Tn", t_hid_fb[fb - 2] if fb >= 2
                               else t_hid_prev[p])] +
                             ([("lg", lg_prologue)] if (c == 0 and fb < 2)
                              else []),
                       inc="ld", amt=16)
                ld_w = S.c("ld")
                pswar = v_ffn_fb[fb - 2] if fb >= 2 else v_ffn_prev[p]
                hbank = psE if p == 0 else psA
                t_hid = 0
                for k in range(NJ):
                    t_hid = S.emit(
                        "tensor",
                        (lambda e, k=k, p=p, hbank=hbank: e.matmul(
                            hbank[:, 0:256],
                            w1buf[:, p * 1024 + k * 128:
                                  p * 1024 + (k + 1) * 128],
                            xnbf[:, k * TC:(k + 1) * TC],
                            start=(k == 0), stop=(k == NJ - 1),
                            skip_group_check=True)),
                        waits=([("ld", ld_w), ("V", v_xn),
                                ("V", pswar)] if k == 0 else []),
                        inc=("Tn" if k == NJ - 1 else None))
                t_hid_fb[fb] = t_hid
                S.emit("scalar",
                       (lambda e, fb=fb, p=p, hbank=hbank: e.activation(
                           fs_s[:, p * 256:p * 256 + 256],
                           hbank[:, 0:256], AF.Sin,
                           bias=vec_s[:, fb:fb + 1], scale=0.1)),
                       waits=[("Tn", t_hid), ("V", pswar)])
                S.emit("scalar",
                       (lambda e, fb=fb, p=p, hbank=hbank: e.activation(
                           fs_c[:, p * 256:p * 256 + 256],
                           hbank[:, 0:256], AF.Sin,
                           bias=vec_c[:, fb:fb + 1], scale=0.1)))
                a_sc = S.emit("scalar",
                              (lambda e, fb=fb, p=p: e.activation(
                                  fs_c[:, p * 256:p * 256 + 256],
                                  fs_c[:, p * 256:p * 256 + 256], AF.Copy,
                                  bias=0.0, scale=P1[:, fb:fb + 1])),
                              inc="A")
                S.emit("vector",
                       (lambda e, fb=fb, p=p: e.scalar_tensor_tensor(
                           fs_s[:, p * 256:p * 256 + 256],
                           fs_s[:, p * 256:p * 256 + 256],
                           P2[:, fb:fb + 1],
                           fs_c[:, p * 256:p * 256 + 256],
                           ALU.mult, ALU.add)),
                       waits=[("A", a_sc)])
                v_ffn_fb[fb] = S.emit(
                    "vector",
                    (lambda e, fb=fb, p=p, hbank=hbank: e.scalar_tensor_tensor(
                        fs_s[:, p * 256:p * 256 + 256],
                        hbank[:, 0:256],
                        b1col_s[:, fb:fb + 1],
                        fs_s[:, p * 256:p * 256 + 256],
                        ALU.add, ALU.add)),
                    inc="V")
                a_relu = S.emit("scalar",
                                (lambda e, fb=fb, p=p: e.activation(
                                    act[:, fb * TC:(fb + 1) * TC],
                                    fs_s[:, p * 256:p * 256 + 256], AF.Relu,
                                    bias=0.0, scale=1.0)),
                                waits=[("V", v_ffn_fb[fb])], inc="A")
            t_hid_prev = [t_hid_fb[NFB - 2], t_hid_fb[NFB - 1]]
            v_ffn_prev = [v_ffn_fb[NFB - 2], v_ffn_fb[NFB - 1]]

            s_asm = delta
            t_w2_J = [0] * NJ
            v_asm_J = [0] * NJ
            for j in range(NJ):
                pj = j % 2
                S.emit("sync",
                       (lambda e, j=j, pj=pj: e.dma_start(
                           out=w2buf[:, pj * 4096:(pj + 1) * 4096],
                           in_=w2X[:, j * 4096:(j + 1) * 4096])),
                       waits=[("Tn", t_w2_J[j - 2] if j >= 2
                               else t_w2_prev[pj])] +
                             ([("lg", lg_prologue)] if (c == 0 and j < 2)
                              else []),
                       inc="ld", amt=16)
                ld_w2 = S.c("ld")
                bank = psF[2 + pj]
                aswar = v_asm_J[j - 2] if j >= 2 else v_asm_prevJ[pj]
                for fb in range(NFB):
                    t_w2_J[j] = S.emit(
                        "tensor",
                        (lambda e, j=j, fb=fb, pj=pj, bank=bank: e.matmul(
                            bank[:, 0:256],
                            w2buf[:, pj * 4096 + fb * 128:
                                  pj * 4096 + (fb + 1) * 128],
                            act[:, fb * TC:(fb + 1) * TC],
                            start=(fb == 0), stop=(fb == NFB - 1),
                            skip_group_check=True)),
                        waits=([("ld", ld_w2), ("A", a_relu),
                                ("V", aswar)] if fb == 0 else []),
                        inc=("Tn" if fb == NFB - 1 else None))
                S.emit("vector",
                       (lambda e, j=j, x_c=x_c, s_asm=s_asm:
                        e.scalar_tensor_tensor(
                            s_asm[:, j * TC:(j + 1) * TC],
                            xn2[:, j * TC:(j + 1) * TC],
                            dskcol_s[:, j:j + 1],
                            x_c[:, j * TC:(j + 1) * TC],
                            ALU.mult, ALU.add)),
                       waits=([("A", a_dA_last),
                               ("lg", lg_store[c - 2] if c >= 2 else 0)]
                              if j == 0 else []))
                S.emit("vector",
                       (lambda e, j=j, s_asm=s_asm: e.tensor_add(
                           s_asm[:, j * TC:(j + 1) * TC],
                           s_asm[:, j * TC:(j + 1) * TC],
                           xn[:, j * TC:(j + 1) * TC])))
                S.emit("vector",
                       (lambda e, j=j, s_asm=s_asm: e.tensor_add(
                           s_asm[:, j * TC:(j + 1) * TC],
                           s_asm[:, j * TC:(j + 1) * TC],
                           y_sb[:, j * TC:(j + 1) * TC])))
                v_asm_J[j] = S.emit(
                    "vector",
                    (lambda e, j=j, s_asm=s_asm, out_c=out_c, bank=bank:
                     e.scalar_tensor_tensor(
                        out_c[:, j * TC:(j + 1) * TC],
                        bank[:, 0:256],
                        b2col_s[:, j:j + 1],
                        s_asm[:, j * TC:(j + 1) * TC],
                        ALU.add, ALU.add)),
                    waits=[("Tn", t_w2_J[j])], inc="V")
            t_w2_prev = [t_w2_J[NJ - 2], t_w2_J[NJ - 1]]
            v_asm_prevJ = [v_asm_J[NJ - 2], v_asm_J[NJ - 1]]
            v_asm[c] = v_asm_J[NJ - 1]
            v_xn_prev = v_xn
            v_xn2_prev = v_xn2

            lg_store[c] = S.emit(
                "gpsimd",
                (lambda e, c=c, out_c=out_c: e.dma_start(
                    out=outT_r[:, :, c * TC:(c + 1) * TC],
                    in_=out_c.rearrange("p (j t) -> p j t", j=NJ))),
                waits=[("V", v_asm[c])], inc="lg", amt=16)

        S.emit("gpsimd", lambda e: None, waits=[("lg", S.c("lg"))])

        @block.sync
        def _(eng):
            S.replay("sync", eng)

        @block.vector
        def _(eng):
            S.replay("vector", eng)

        @block.scalar
        def _(eng):
            S.replay("scalar", eng)

        @block.tensor
        def _(eng):
            S.replay("tensor", eng)

        @block.gpsimd
        def _(eng):
            S.replay("gpsimd", eng)

    return nc


def kernel(**inputs):
    x = np.asarray(inputs["L0_attn_x"], dtype=np.float32)
    gamma = np.asarray(inputs["L0_attn_gamma"], dtype=np.float32)
    beta = np.asarray(inputs["L0_attn_beta"], dtype=np.float32)
    W_delta = np.asarray(inputs["L0_attn_W_delta"], dtype=np.float32)
    b_delta = np.asarray(inputs["L0_attn_b_delta"], dtype=np.float32)
    W_B = np.asarray(inputs["L0_attn_W_B"], dtype=np.float32)
    b_B = np.asarray(inputs["L0_attn_b_B"], dtype=np.float32)
    W_C = np.asarray(inputs["L0_attn_W_C"], dtype=np.float32)
    b_C = np.asarray(inputs["L0_attn_b_C"], dtype=np.float32)
    A_log = np.asarray(inputs["L0_attn_A_log"], dtype=np.float32)
    D_skip = np.asarray(inputs["L0_attn_D_skip"], dtype=np.float32)
    W1 = np.asarray(inputs["L0_ffn_W1"], dtype=np.float32)
    b1 = np.asarray(inputs["L0_ffn_b1"], dtype=np.float32)
    qp = np.asarray(inputs["L0_ffn_quantum_phases"], dtype=np.float32)
    wr = np.asarray(inputs["L0_ffn_quantum_weights_real"], dtype=np.float32)
    wi = np.asarray(inputs["L0_ffn_quantum_weights_imag"], dtype=np.float32)
    W2 = np.asarray(inputs["L0_ffn_W2"], dtype=np.float32)
    b2 = np.asarray(inputs["L0_ffn_b2"], dtype=np.float32)

    def col8(v):
        return np.ascontiguousarray(v.reshape(NJ, 128).T)

    def col32(v):
        return np.ascontiguousarray(v.reshape(NFB, 128).T)

    alogp = np.ascontiguousarray(
        A_log.reshape(NJ, 128, NST).transpose(1, 0, 2).reshape(128, 128))
    wbp = np.ascontiguousarray(
        W_B.reshape(NJ, 128, NST).transpose(1, 0, 2).reshape(128, 128))
    wcp = np.ascontiguousarray(
        W_C.reshape(NJ, 128, NST).transpose(1, 0, 2).reshape(128, 128))
    bcbiasB = np.ascontiguousarray(b_B.reshape(16, 1))
    bcbiasC = np.ascontiguousarray(b_C.reshape(16, 1))
    selcat = np.zeros((16, 16 * 128), np.float32)
    for n in range(16):
        selcat[n, n * 128:(n + 1) * 128] = 1.0

    shared = dict(
        wd=np.ascontiguousarray(W_delta), w1=np.ascontiguousarray(W1),
        w2=np.ascontiguousarray(W2), wbp=wbp, wcp=wcp, alogp=alogp,
        gcol=col8(gamma), bcol=col8(beta), bdcol=col8(b_delta),
        dskcol=col8(D_skip), b2col=col8(b2), b1col=col32(b1),
        qpcol=col32(qp), wrcol=col32(wr), wicol=col32(wi),
        bcbiasB=bcbiasB, bcbiasC=bcbiasC, selcat=selcat,
        eye_in=np.eye(128, dtype=np.float32),
        onescol_in=np.ones((128, 1), np.float32),
        onesrow=np.ones((1, 128), np.float32),
    )
    in_maps = []
    for b in range(8):
        m = dict(shared)
        m["xT"] = np.ascontiguousarray(x[b].T)
        in_maps.append(m)

    nc = build_nc()
    res = run_bass_kernel_spmd(nc, in_maps, core_ids=list(range(8)))
    out = np.empty((8, T, D), np.float32)
    for b in range(8):
        out[b] = res.results[b]["outT"].T
    return out


# version 25
# speedup vs baseline: 1.1191x; 1.1191x over previous
import math
from contextlib import ExitStack

import numpy as np

import concourse.bass as bass
import concourse.mybir as mybir
from concourse.bass_utils import run_bass_kernel_spmd

F32 = mybir.dt.float32
BF16 = mybir.dt.bfloat16
I32 = mybir.dt.int32
AF = mybir.ActivationFunctionType
ALU = mybir.AluOpType

T = 4096
D = 1024
NST = 16
TC = 256
NCH = T // TC
NJ = D // 128
SEG = TC + 2
SCW = NST * SEG
DFF = 4096
NFB = DFF // 128
EPS = 1e-5
TWO_PI_SAFE = 6.2831845


class Sched:

    ENGINES = ("sync", "vector", "scalar", "tensor", "gpsimd")

    def __init__(self, nc, es):
        self.nc = nc
        self.ops = {e: [] for e in self.ENGINES}
        self.sem_h = {}
        self.cnt = {}
        for name in ("ld", "lg", "V", "A", "Tn"):
            self.sem_h[name] = es.enter_context(nc.semaphore(name))
            self.cnt[name] = 0

    def emit(self, eng, fn, waits=(), inc=None, amt=1):
        wl = {}
        for s in waits:
            name, val = s
            if val > 0:
                wl[name] = max(wl.get(name, 0), val)
        self.ops[eng].append((sorted(wl.items()), fn, inc, amt))
        if inc is not None:
            self.cnt[inc] += amt
        return self.cnt[inc] if inc is not None else None

    def c(self, sem):
        return self.cnt[sem]

    def replay(self, ename, eng):
        need_drain = ename == "vector"
        first = True
        for waits, fn, inc, amt in self.ops[ename]:
            if need_drain and not first:
                eng.drain()
            first = False
            for s, v in waits:
                eng.wait_ge(self.sem_h[s], v)
            inst = fn(eng)
            if inc is not None:
                if inst is not None:
                    inst.then_inc(self.sem_h[inc], amt)
                else:
                    eng.sem_inc(self.sem_h[inc], amt)


def build_nc():
    nc = bass.Bass("TRN2", target_bir_lowering=False,
                   detect_race_conditions=False)

    def din(name, shape, dt=F32):
        return nc.dram_tensor(name, shape, dt, kind="ExternalInput").ap()

    xT = din("xT", [D, T])
    wd = din("wd", [D, D])
    w1 = din("w1", [D, DFF])
    w2 = din("w2", [DFF, D])
    wbp = din("wbp", [128, 128])
    wcp = din("wcp", [128, 128])
    alogp = din("alogp", [128, 128])
    gcol = din("gcol", [128, NJ])
    bcol = din("bcol", [128, NJ])
    bdcol = din("bdcol", [128, NJ])
    dskcol = din("dskcol", [128, NJ])
    b2col = din("b2col", [128, NJ])
    b1col = din("b1col", [128, NFB])
    qpcol = din("qpcol", [128, NFB])
    wrcol = din("wrcol", [128, NFB])
    wicol = din("wicol", [128, NFB])
    bcbiasB = din("bcbiasB", [16, 1])
    bcbiasC = din("bcbiasC", [16, 1])
    selcat = din("selcat", [16, 16 * 128])
    eye_in = din("eye_in", [128, 128])
    onescol_in = din("onescol_in", [128, 1])
    onesrow = din("onesrow", [1, 128])

    outT = nc.dram_tensor("outT", [D, T], F32, kind="ExternalOutput").ap()
    w1X = nc.dram_tensor("w1X", [128, NFB * 1024], BF16, kind="Internal").ap()
    w2X = nc.dram_tensor("w2X", [128, NFB * 1024], BF16, kind="Internal").ap()

    es = ExitStack()
    with es:
        def sb(name, shape, dt=F32):
            return es.enter_context(nc.sbuf_tensor(name, shape, dt))

        xt = sb("xt", [128, 2 * NJ * TC])
        out_t = sb("out_t", [128, 2 * NJ * TC])
        xn = sb("xn", [128, NJ * TC])
        xnbf = sb("xnbf", [128, NJ * TC], BF16)
        xn2 = sb("xn2", [128, NJ * TC])
        xn2bf = sb("xn2bf", [128, NJ * TC], BF16)
        delta = sb("delta", [128, NJ * TC])
        ubf = sb("ubf", [128, NJ * TC], BF16)
        dAb = sb("dAb", [128, SCW])
        dAb2 = sb("dAb2", [128, SCW])
        wb = sb("wb", [128, SCW], BF16)
        Hb = sb("Hb", [128, SCW], BF16)
        Bexp = sb("Bexp", [128, NST * TC], BF16)
        Cexp = sb("Cexp", [128, NST * TC], BF16)
        stats = sb("stats", [1, 512])
        hstate = sb("hstate", [128, 128], BF16)
        A_pack = sb("A_pack", [128, 128])
        WBbf = sb("WBbf", [128, 128], BF16)
        WCbf = sb("WCbf", [128, 128], BF16)
        Wdbf = sb("Wdbf", [128, NJ * 1024], BF16)
        w1buf = sb("w1buf", [128, 2 * 1024], BF16)
        w2buf = sb("w2buf", [128, 2 * 4096], BF16)
        y_sb = sb("y_sb", [128, NJ * TC], BF16)
        act = sb("act", [128, NFB * TC], BF16)
        fs_s = sb("fs_s", [128, 2 * TC])
        fs_c = sb("fs_c", [128, 2 * TC])
        eye_bf = sb("eye_bf", [128, 128], BF16)
        onescol_bf = sb("onescol_bf", [128, 1], BF16)
        bcmB = sb("bcmB", [16, TC], BF16)
        bcmC = sb("bcmC", [16, TC], BF16)
        cols5 = sb("cols5", [128, 5 * NJ])
        gcol_s = cols5[:, 0 * NJ:1 * NJ]
        bcol_s = cols5[:, 1 * NJ:2 * NJ]
        bdcol_s = cols5[:, 2 * NJ:3 * NJ]
        dskcol_s = cols5[:, 3 * NJ:4 * NJ]
        b2col_s = cols5[:, 4 * NJ:5 * NJ]
        b1col_s = sb("b1col_s", [128, NFB])
        vec_s = sb("vec_s", [128, NFB])
        vec_c = sb("vec_c", [128, NFB])
        P1 = sb("P1", [128, NFB])
        P2 = sb("P2", [128, NFB])
        wr_s = fs_c[:, 3 * NFB:4 * NFB]
        wi_s = fs_c[:, 4 * NFB:5 * NFB]
        qp_s = fs_c[:, 0 * NFB:1 * NFB]
        scr1 = fs_c[:, 1 * NFB:2 * NFB]
        scr2 = fs_c[:, 2 * NFB:3 * NFB]
        bcb2 = sb("bcb2", [16, 2])
        bcbB_s = bcb2[:, 0:1]
        bcbC_s = bcb2[:, 1:2]
        onesrow_s = sb("onesrow_s", [1, 128])
        selcat_bf = sb("selcat_bf", [16, 16 * 128], BF16)
        cst = sb("cst", [128, 4])
        thr_s = fs_c[:, 5 * NFB:5 * NFB + 16]

        def ps(name):
            return es.enter_context(nc.psum_tensor(name, [128, 512], F32))
        psA = ps("psA")
        psB = ps("psB")
        psC = ps("psC")
        psE = ps("psE")
        psF = [ps(f"psF{i}") for i in range(4)]

        S = Sched(nc, es)
        block = es.enter_context(nc.Block())

        xT_r = xT.rearrange("(j p) t -> p j t", p=128)
        outT_r = outT.rearrange("(j p) t -> p j t", p=128)

        def dma(out_ap, in_ap):
            def f(e, o=out_ap, i=in_ap):
                return e.dma_start(out=o, in_=i)
            return f

        for o, i in [
            (A_pack[:, :], alogp), (gcol_s[:, :], gcol), (bcol_s[:, :], bcol),
            (bdcol_s[:, :], bdcol), (dskcol_s[:, :], dskcol),
            (b2col_s[:, :], b2col), (b1col_s[:, :], b1col), (qp_s[:, :], qpcol),
            (wr_s[:, :], wrcol), (bcbB_s[:, :], bcbiasB),
            (bcbC_s[:, :], bcbiasC), (wi_s[:, :], wicol),
            (onesrow_s[:, :], onesrow),
        ]:
            S.emit("sync", dma(o, i), inc="ld", amt=16)
        S.emit("sync", dma(dAb[:16, 0:2048], selcat), inc="ld", amt=16)
        S.emit("sync", dma(dAb[:, 2048:2176], eye_in), inc="ld", amt=16)
        S.emit("sync", dma(dAb[:, 2176:2177], onescol_in), inc="ld", amt=16)
        S.emit("sync", dma(dAb[:, 2432:2560], wbp), inc="ld", amt=16)
        S.emit("sync", dma(dAb[:, 2560:2688], wcp), inc="ld", amt=16)
        ld_small = S.c("ld")
        S.emit("vector", lambda e: e.memset(cst[:, 0:1], EPS))
        S.emit("vector", lambda e: e.memset(cst[:, 1:2], 1.0))
        S.emit("vector", lambda e: e.memset(cst[:, 2:3], 0.0))
        S.emit("vector", lambda e: e.memset(cst[:, 3:4], 1.5707962))
        for jj in range(16):
            b = -(jj + 0.5) if jj < 8 else (jj - 8 + 0.5)
            S.emit("vector", (lambda e, jj=jj, b=b: e.memset(
                thr_s[:, jj:jj + 1], b)))
        S.emit("vector", lambda e: e.tensor_copy(selcat_bf[:, :], dAb[:16, 0:2048]),
               waits=[("ld", ld_small)])
        S.emit("vector", lambda e: e.tensor_copy(eye_bf[:, :], dAb[:, 2048:2176]))
        S.emit("vector", lambda e: e.tensor_copy(onescol_bf[:, :], dAb[:, 2176:2177]))
        S.emit("vector", lambda e: e.tensor_copy(WBbf[:, :], dAb[:, 2432:2560]))
        v_consts = S.emit("vector",
                          lambda e: e.tensor_copy(WCbf[:, :], dAb[:, 2560:2688]),
                          inc="V")

        a_exp = S.emit("scalar",
                       lambda e: e.activation(A_pack[:, :], A_pack[:, :], AF.Exp,
                                              bias=0.0, scale=1.0),
                       waits=[("ld", ld_small)], inc="A")
        S.emit("vector", lambda e: e.tensor_scalar_mul(A_pack[:, :], A_pack[:, :],
                                                       -1.0),
               waits=[("A", a_exp)])
        v_y = S.emit("vector",
                     lambda e: e.tensor_scalar_mul(scr1[:, :], qp_s[:, :],
                                                   1.0 / (2 * math.pi)),
                     inc="V")
        for jj in range(16):
            b = -(jj + 0.5) if jj < 8 else (jj - 8 + 0.5)
            S.emit("scalar",
                   (lambda e, jj=jj: e.activation(
                       fs_s[:, jj * NFB:(jj + 1) * NFB], scr1[:, :], AF.Sign,
                       bias=thr_s[:, jj:jj + 1], scale=1.0)),
                   waits=[("V", v_y)] if jj == 0 else [],
                   inc=("A" if jj == 15 else None))
        a_signs = S.c("A")
        S.emit("vector", lambda e: e.tensor_copy(scr2[:, :], fs_s[:, 0:NFB]),
               waits=[("A", a_signs)])
        for jj in range(1, 16):
            S.emit("vector", (lambda e, jj=jj: e.tensor_add(
                scr2[:, :], scr2[:, :], fs_s[:, jj * NFB:(jj + 1) * NFB])))
        S.emit("vector", lambda e: e.tensor_scalar_mul(scr2[:, :], scr2[:, :],
                                                       0.5))
        v_frac = S.emit("vector",
                        lambda e: e.tensor_sub(scr1[:, :], scr1[:, :], scr2[:, :]),
                        inc="V")
        S.emit("scalar", lambda e: e.activation(scr2[:, :], scr1[:, :], AF.Abs,
                                                bias=0.0, scale=1.0),
               waits=[("V", v_frac)])
        S.emit("scalar", lambda e: e.activation(P1[:, :], scr1[:, :], AF.Sin,
                                                bias=0.0, scale=TWO_PI_SAFE))
        a_sc2 = S.emit("scalar",
                       lambda e: e.activation(P2[:, :], scr2[:, :], AF.Sin,
                                              bias=cst[:, 3:4],
                                              scale=-TWO_PI_SAFE),
                       inc="A")
        S.emit("vector", lambda e: e.tensor_mul(scr1[:, :], P2[:, :], wr_s[:, :]),
               waits=[("A", a_sc2)])
        S.emit("vector", lambda e: e.tensor_mul(scr2[:, :], P1[:, :], wi_s[:, :]))
        S.emit("vector", lambda e: e.tensor_add(scr1[:, :], scr1[:, :], scr2[:, :]))
        S.emit("vector", lambda e: e.tensor_mul(scr1[:, :], scr1[:, :], wi_s[:, :]))
        S.emit("vector", lambda e: e.tensor_scalar_mul(scr1[:, :], scr1[:, :], 0.1))
        S.emit("vector", lambda e: e.tensor_mul(P1[:, :], P1[:, :], scr1[:, :]))
        S.emit("vector", lambda e: e.tensor_mul(P2[:, :], P2[:, :], scr1[:, :]))
        S.emit("vector", lambda e: e.tensor_scalar_mul(vec_s[:, :], b1col_s[:, :],
                                                       0.1))
        S.emit("vector", lambda e: e.tensor_scalar(vec_c[:, :], b1col_s[:, :], 0.1,
                                                   math.pi / 2.0, ALU.mult,
                                                   ALU.add),
               inc="V")

        ld_x = [S.emit("sync",
                       dma(xt[:, 0:NJ * TC].rearrange("p (j t) -> p j t", j=NJ),
                           xT_r[:, :, 0:TC]), inc="ld", amt=16)]

        wd_r = wd.rearrange("(k p) d -> p k d", p=128)
        w1_r = w1.rearrange("(k p) f -> p k f", p=128)
        w2_r = w2.rearrange("(fb p) d -> p fb d", p=128)
        w1X_r = w1X.rearrange("p (fb j c) -> p fb j c", fb=NFB, j=NJ)

        stage_f = [dAb[:, 0:2048], dAb[:, 2048:4096]]
        stage_b = [wb[:, 0:2048], wb[:, 2048:4096]]
        v_cast_prev = {0: v_consts, 1: v_consts}
        lg_out_prev = {0: 0, 1: 0}

        for k in range(NJ):
            p = k % 2
            ldv = S.emit("sync", dma(stage_f[p][:, 0:1024], wd_r[:, k, :]),
                         waits=[("V", v_cast_prev[p])], inc="ld", amt=16)
            v_cast_prev[p] = S.emit(
                "vector",
                (lambda e, k=k, p=p: e.tensor_copy(
                    Wdbf[:, k * 1024:(k + 1) * 1024], stage_f[p][:, 0:1024])),
                waits=[("ld", ldv)], inc="V")
        w2X_r = w2X.rearrange("p (j fb c) -> p j fb c", j=NJ, fb=NFB)
        for fb in range(NFB):
            p = fb % 2
            ldv = S.emit("sync", dma(stage_f[p][:, 0:1024], w2_r[:, fb, :]),
                         waits=[("V", v_cast_prev[p])], inc="ld", amt=16)
            v_cast_prev[p] = S.emit(
                "vector",
                (lambda e, p=p: e.tensor_copy(stage_b[p][:, 0:1024],
                                              stage_f[p][:, 0:1024])),
                waits=[("ld", ldv), ("lg", lg_out_prev[p])], inc="V")
            lg_out_prev[p] = S.emit(
                "gpsimd",
                (lambda e, fb=fb, p=p: e.dma_start(
                    out=w2X_r[:, :, fb, :],
                    in_=stage_b[p][:, 0:1024].rearrange(
                        "p (j c) -> p j c", j=NJ))),
                waits=[("V", v_cast_prev[p])], inc="lg", amt=16)
        for k in range(NJ):
            for h in range(2):
                p = h
                ldv = S.emit("sync",
                             dma(stage_f[p][:, 0:2048],
                                 w1_r[:, k, h * 2048:(h + 1) * 2048]),
                             waits=[("V", v_cast_prev[p])], inc="ld", amt=16)
                v_cast_prev[p] = S.emit(
                    "vector",
                    (lambda e, p=p: e.tensor_copy(stage_b[p][:, 0:2048],
                                                  stage_f[p][:, 0:2048])),
                    waits=[("ld", ldv), ("lg", lg_out_prev[p])], inc="V")
                lg_out_prev[p] = S.emit(
                    "gpsimd",
                    (lambda e, k=k, h=h, p=p: e.dma_start(
                        out=w1X_r[:, h * 16:(h + 1) * 16, k, :],
                        in_=stage_b[p][:, 0:2048].rearrange(
                            "p (fb c) -> p fb c", fb=16))),
                    waits=[("V", v_cast_prev[p])], inc="lg", amt=16)
        lg_prologue = S.c("lg")

        S.emit("vector", lambda e: e.memset(dAb[:, :], 0.0),
               waits=[("lg", lg_prologue)])
        S.emit("vector", lambda e: e.memset(dAb2[:, :], 0.0))
        S.emit("vector", lambda e: e.memset(wb[:, :], 0.0))
        v_zeroed = S.emit("vector", lambda e: e.memset(hstate[:, :], 0.0),
                          inc="V")

        dA_seg2 = [dAb[:].rearrange("p (n s) -> p n s", n=NST),
                   dAb2[:].rearrange("p (n s) -> p n s", n=NST)]
        dA_flat2 = [dAb, dAb2]
        wb_seg = wb[:].rearrange("p (n s) -> p n s", n=NST)
        Hb_seg = Hb[:].rearrange("p (n s) -> p n s", n=NST)
        Bexp_seg = Bexp[:].rearrange("p (n t) -> p n t", n=NST)
        Cexp_seg = Cexp[:].rearrange("p (n t) -> p n t", n=NST)

        st_mu = stats[:, 0:256]
        st_msq = stats[:, 256:512]
        st_tmp = fs_s[0:1, 0:256]
        st_rstd = fs_s[0:1, 0:256]

        v_asm = [0] * (NCH + 1)
        lg_store = [0] * (NCH + 1)
        a_expevac_prev = 0
        t_y_prev_chunk = 0
        v_xn_prev = 0
        v_xn2_prev = 0
        v_bcm_prev = 0
        t_hid_prev = [0, 0]
        t_w2_prev = [0, 0]
        v_ffn_prev = [0, 0]
        v_asm_prevJ = [0, 0]
        a_yevac_prev = [0, 0]
        v_scan_prev_c = [0, 0]
        a_dA_last = 0

        for c in range(NCH):
            pc = c % 2
            xo = pc * NJ * TC
            x_c = xt[:, xo:xo + NJ * TC]
            out_c = out_t[:, xo:xo + NJ * TC]
            xbf = ubf
            sqbf = Hb[:, 0:NJ * TC]

            if c + 1 < NCH:
                ld_x.append(S.emit(
                    "sync",
                    (lambda e, c=c: e.dma_start(
                        out=xt[:, ((c + 1) % 2) * NJ * TC:
                               (((c + 1) % 2) + 1) * NJ * TC].rearrange(
                            "p (j t) -> p j t", j=NJ),
                        in_=xT_r[:, :, (c + 1) * TC:(c + 2) * TC])),
                    waits=[("V", v_asm[c - 1] if c >= 1 else 0)],
                    inc="ld", amt=16))

            v_xbf = S.emit("vector",
                           (lambda e, x_c=x_c: e.tensor_copy(xbf[:, :], x_c)),
                           waits=[("ld", ld_x[c]), ("V", v_zeroed)], inc="V")
            a_sq = S.emit("scalar",
                          lambda e: e.activation(sqbf, xbf[:, :], AF.Square,
                                                 bias=0.0, scale=1.0),
                          waits=[("V", v_xbf), ("Tn", t_y_prev_chunk)], inc="A")
            for j in range(NJ):
                S.emit("tensor",
                       (lambda e, j=j: e.matmul(
                           psA[0:1, 0:256], onescol_bf[:, :],
                           xbf[:, j * TC:(j + 1) * TC],
                           start=(j == 0), stop=(j == NJ - 1),
                           skip_group_check=True)),
                       waits=([("V", v_xbf), ("V", v_xn_prev),
                               ("A", a_expevac_prev)] if j == 0 else []))
            t_sums = 0
            for j in range(NJ):
                t_sums = S.emit(
                    "tensor",
                    (lambda e, j=j: e.matmul(
                        psA[0:1, 256:512], onescol_bf[:, :],
                        sqbf[:, j * TC:(j + 1) * TC],
                        start=(j == 0), stop=(j == NJ - 1),
                        skip_group_check=True)),
                    waits=[("A", a_sq)] if j == 0 else [],
                    inc=("Tn" if j == NJ - 1 else None))
            S.emit("vector", lambda e: e.tensor_scalar_mul(st_mu, psA[0:1, 0:256],
                                                           1.0 / D),
                   waits=[("Tn", t_sums)])
            S.emit("vector", lambda e: e.tensor_scalar_mul(st_msq,
                                                           psA[0:1, 256:512],
                                                           1.0 / D))
            S.emit("vector", lambda e: e.tensor_mul(st_tmp, st_mu, st_mu))
            v_var = S.emit("vector",
                           lambda e: e.tensor_sub(st_msq, st_msq, st_tmp),
                           inc="V")
            a_rstd = S.emit("scalar",
                            lambda e: e.activation(st_rstd, st_msq, AF.Sqrt,
                                                   bias=cst[0:1, 0:1], scale=1.0),
                            waits=[("V", v_var)], inc="A")
            S.emit("vector", lambda e: e.reciprocal(st_rstd, st_rstd),
                   waits=[("A", a_rstd)])
            v_beta = S.emit("vector",
                            lambda e: e.scalar_tensor_tensor(
                                st_mu, st_mu, -1.0, st_rstd,
                                ALU.mult, ALU.mult),
                            inc="V")
            S.emit("tensor", lambda e: e.matmul(psA[:, 256:512], onesrow_s[:, :],
                                                st_rstd, start=True, stop=True,
                                                skip_group_check=True),
                   waits=[("V", v_beta), ("A", a_expevac_prev)])
            t_eab = S.emit("tensor",
                           lambda e: e.matmul(psA[:, 0:256], onesrow_s[:, :],
                                              st_mu, start=True, stop=True,
                                              skip_group_check=True),
                           inc="Tn")
            for j in range(NJ):
                S.emit("vector",
                       (lambda e, j=j, x_c=x_c: e.tensor_mul(
                           xn[:, j * TC:(j + 1) * TC],
                           x_c[:, j * TC:(j + 1) * TC], psA[:, 256:512])),
                       waits=[("Tn", t_eab)] if j == 0 else [])
                S.emit("vector",
                       (lambda e, j=j: e.tensor_add(
                           xn[:, j * TC:(j + 1) * TC],
                           xn[:, j * TC:(j + 1) * TC], psA[:, 0:256])))
            v_xn = S.emit("vector",
                          lambda e: e.tensor_copy(xnbf[:, :], xn[:, :]), inc="V")

            a_sq2 = S.emit("scalar",
                           lambda e: e.activation(sqbf, xnbf[:, :], AF.Square,
                                                  bias=0.0, scale=1.0),
                           waits=[("V", v_xn)], inc="A")
            t_sum2 = 0
            for j in range(NJ):
                t_sum2 = S.emit(
                    "tensor",
                    (lambda e, j=j: e.matmul(
                        psC[0:1, 0:256], onescol_bf[:, :],
                        sqbf[:, j * TC:(j + 1) * TC],
                        start=(j == 0), stop=(j == NJ - 1),
                        skip_group_check=True)),
                    waits=([("A", a_sq2), ("V", v_xn2_prev),
                            ("V", v_bcm_prev)] if j == 0 else []),
                    inc=("Tn" if j == NJ - 1 else None))
            v_m2 = S.emit("vector",
                          lambda e: e.tensor_scalar_mul(st_mu, psC[0:1, 0:256],
                                                        1.0 / D),
                          waits=[("Tn", t_sum2)], inc="V")
            a_rstd2 = S.emit("scalar",
                             lambda e: e.activation(st_msq, st_mu, AF.Sqrt,
                                                    bias=cst[0:1, 0:1], scale=1.0),
                             waits=[("V", v_m2)], inc="A")
            v_rstd2 = S.emit("vector", lambda e: e.reciprocal(st_msq, st_msq),
                             waits=[("A", a_rstd2)], inc="V")
            t_er2 = S.emit("tensor",
                           lambda e: e.matmul(psC[:, 256:512], onesrow_s[:, :],
                                              st_msq, start=True, stop=True,
                                              skip_group_check=True),
                           waits=[("V", v_rstd2)], inc="Tn")
            for j in range(NJ):
                S.emit("vector",
                       (lambda e, j=j: e.scalar_tensor_tensor(
                           xn2[:, j * TC:(j + 1) * TC],
                           xn[:, j * TC:(j + 1) * TC],
                           gcol_s[:, j:j + 1], psC[:, 256:512],
                           ALU.mult, ALU.mult)),
                       waits=[("Tn", t_er2)] if j == 0 else [])
                S.emit("vector",
                       (lambda e, j=j: e.tensor_scalar_add(
                           xn2[:, j * TC:(j + 1) * TC],
                           xn2[:, j * TC:(j + 1) * TC], bcol_s[:, j:j + 1])))
            v_xn2 = S.emit("vector",
                           lambda e: e.tensor_copy(xn2bf[:, :], xn2[:, :]),
                           inc="V")

            a_sp_J = [0] * NJ
            a_exp_J = [0] * NJ
            for j in range(NJ):
                bank = psB if j % 2 == 0 else psC
                half = ((j // 2) % 2) * 256
                war = [("A", a_exp_J[j - 2])] if j >= 2 else \
                    [("A", a_expevac_prev)]
                t_dj = 0
                for k in range(NJ):
                    t_dj = S.emit(
                        "tensor",
                        (lambda e, j=j, k=k, bank=bank, half=half: e.matmul(
                            bank[:, half:half + 256],
                            Wdbf[:, (k * NJ + j) * 128:(k * NJ + j + 1) * 128],
                            xn2bf[:, k * TC:(k + 1) * TC],
                            start=(k == 0), stop=(k == NJ - 1),
                            skip_group_check=True)),
                        waits=([("V", v_xn2)] + war if k == 0 else []),
                        inc=("Tn" if k == NJ - 1 else None))
                a_exp_J[j] = S.emit(
                    "scalar",
                    (lambda e, j=j, bank=bank, half=half: e.activation(
                        fs_s[:, 0:256], bank[:, half:half + 256], AF.Exp,
                        bias=bdcol_s[:, j:j + 1], scale=1.0)),
                    waits=[("Tn", t_dj)], inc="A")
                a_sp_J[j] = S.emit(
                    "scalar",
                    (lambda e, j=j: e.activation(
                        delta[:, j * TC:(j + 1) * TC], fs_s[:, 0:256], AF.Ln,
                        bias=cst[:, 1:2], scale=1.0)),
                    inc="A")

            t_bc = 0
            for (w_t, cols) in ((WBbf, 0), (WCbf, 256)):
                for k in range(NJ):
                    t_bc = S.emit(
                        "tensor",
                        (lambda e, w_t=w_t, cols=cols, k=k: e.matmul(
                            psC[0:16, cols:cols + 256],
                            w_t[:, k * 16:(k + 1) * 16],
                            xn2bf[:, k * TC:(k + 1) * TC],
                            start=(k == 0), stop=(k == NJ - 1),
                            skip_group_check=True)),
                        waits=([("V", v_xn2), ("A", a_exp_J[NJ - 1])]
                               if k == 0 else []),
                        inc=("Tn" if (k == NJ - 1 and cols == 256) else None))
            S.emit("vector", lambda e: e.tensor_scalar_add(
                bcmB[:, :], psC[0:16, 0:256], bcbB_s[:, :]),
                   waits=[("Tn", t_bc)])
            v_bcm = S.emit("vector",
                           lambda e: e.tensor_scalar_add(
                               bcmC[:, :], psC[0:16, 256:512],
                               bcbC_s[:, :]),
                           inc="V")
            v_bcm_prev = v_bcm

            v_u = S.emit("vector",
                         lambda e: e.tensor_mul(ubf[:, :], delta[:, :],
                                                xn2[:, :]),
                         waits=[("A", a_sp_J[NJ - 1]), ("Tn", t_sums)], inc="V")

            v_wz_prev = v_asm[c - 1] if c >= 1 else 0
            a_piece = []
            for (dst_seg, bcsrc) in ((Bexp_seg, bcmB), (Cexp_seg, bcmC)):
                for n in range(NST):
                    m = len(a_piece)
                    bank = psB if m % 2 == 0 else psC
                    half = ((m // 2) % 2) * 256
                    t_piece = S.emit(
                        "tensor",
                        (lambda e, n=n, bcsrc=bcsrc, bank=bank, half=half:
                         e.matmul(
                            bank[:, half:half + 256],
                            selcat_bf[:, n * 128:(n + 1) * 128],
                            bcsrc[:, :],
                            start=True, stop=True, skip_group_check=True)),
                        waits=(([("V", v_bcm), ("A", a_sp_J[NJ - 1])]
                                if m <= 1 else []) +
                               ([("A", a_piece[m - 2])] if m >= 2 else [])),
                        inc="Tn")
                    a_piece.append(S.emit(
                        "scalar",
                        (lambda e, n=n, dst_seg=dst_seg, bank=bank, half=half:
                         e.activation(dst_seg[:, n, :],
                                      bank[:, half:half + 256],
                                      AF.Copy, bias=0.0, scale=1.0)),
                        waits=[("Tn", t_piece), ("V", v_wz_prev)], inc="A"))
            a_expevac = a_piece[-1]
            a_expevac_prev = a_expevac

            v_scan_prev = 0
            v_scan_J = [0] * NJ
            t_y_J = 0
            a_yevac_J = [0] * NJ
            for j in range(NJ):
                dwar = (v_scan_J[j - 2] if j >= 2
                        else (v_scan_prev_c[j % 2] if c >= 1 else v_zeroed))
                for n in range(NST):
                    a_dA_last = S.emit(
                        "scalar",
                        (lambda e, j=j, n=n: e.activation(
                            dA_seg2[j % 2][:, n, 2:SEG],
                            delta[:, j * TC:(j + 1) * TC],
                            AF.Exp, bias=0.0,
                            scale=A_pack[:, j * 16 + n:j * 16 + n + 1])),
                        waits=([("V", dwar)] if n == 0 else []),
                        inc=("A" if n == NST - 1 else None))
                S.emit("vector",
                       (lambda e, j=j: e.tensor_copy(
                           wb_seg[:, :, 1:2],
                           hstate[:, j * 16:(j + 1) * 16]
                           .rearrange("p (n o) -> p n o", o=1))),
                       waits=([("A", a_expevac), ("V", v_u)] if j == 0 else []))
                S.emit("vector",
                       (lambda e, j=j: e.tensor_mul(
                           wb_seg[:, :, 2:SEG],
                           ubf[:, j * TC:(j + 1) * TC][:, None, :]
                           .broadcast_to([128, NST, TC]),
                           Bexp_seg[:, :, :])))
                S.emit("vector",
                       (lambda e, j=j: e.tensor_tensor_scan(
                           Hb[:, :], dA_flat2[j % 2][:, :], wb[:, :],
                           cst[:, 2:3], ALU.mult, ALU.add)),
                       waits=[("A", a_dA_last),
                              ("Tn", t_y_J if j > 0 else t_sum2)])
                S.emit("vector",
                       (lambda e, j=j: e.tensor_copy(
                           hstate[:, j * 16:(j + 1) * 16]
                           .rearrange("p (n o) -> p n o", o=1),
                           Hb_seg[:, :, SEG - 1:SEG])))
                v_scan_prev = S.emit(
                    "vector",
                    (lambda e: e.tensor_mul(Hb_seg[:, :, 2:SEG],
                                            Hb_seg[:, :, 2:SEG],
                                            Cexp_seg[:, :, :])),
                    inc="V")
                v_scan_J[j] = v_scan_prev
                ybank = psF[j % 2]
                ywar = a_yevac_J[j - 2] if j >= 2 else a_yevac_prev[j % 2]
                for n in range(NST):
                    t_y_J = S.emit(
                        "tensor",
                        (lambda e, j=j, n=n, ybank=ybank: e.matmul(
                            ybank[:, 0:256],
                            eye_bf[:, :], Hb[:, n * SEG + 2:(n + 1) * SEG],
                            start=(n == 0), stop=(n == NST - 1),
                            skip_group_check=True)),
                        waits=([("V", v_scan_prev), ("A", ywar)]
                               if n == 0 else []),
                        inc=("Tn" if n == NST - 1 else None))
                a_yevac_J[j] = S.emit(
                    "scalar",
                    (lambda e, j=j, ybank=ybank: e.activation(
                        y_sb[:, j * TC:(j + 1) * TC], ybank[:, 0:256],
                        AF.Copy, bias=0.0, scale=1.0)),
                    waits=[("Tn", t_y_J)], inc="A")
            t_y_prev_chunk = t_y_J
            a_yevac_prev = [a_yevac_J[NJ - 2], a_yevac_J[NJ - 1]]
            v_scan_prev_c = [v_scan_J[NJ - 2], v_scan_J[NJ - 1]]

            t_hid_fb = [0] * NFB
            v_ffn_fb = [0] * NFB
            a_relu = 0
            for fb in range(NFB):
                p = fb % 2
                S.emit("sync",
                       (lambda e, fb=fb, p=p: e.dma_start(
                           out=w1buf[:, p * 1024:(p + 1) * 1024],
                           in_=w1X[:, fb * 1024:(fb + 1) * 1024])),
                       waits=[("Tn", t_hid_fb[fb - 2] if fb >= 2
                               else t_hid_prev[p])] +
                             ([("lg", lg_prologue)] if (c == 0 and fb < 2)
                              else []),
                       inc="ld", amt=16)
                ld_w = S.c("ld")
                pswar = v_ffn_fb[fb - 2] if fb >= 2 else v_ffn_prev[p]
                hbank = psE if p == 0 else psA
                t_hid = 0
                for k in range(NJ):
                    t_hid = S.emit(
                        "tensor",
                        (lambda e, k=k, p=p, hbank=hbank: e.matmul(
                            hbank[:, 0:256],
                            w1buf[:, p * 1024 + k * 128:
                                  p * 1024 + (k + 1) * 128],
                            xnbf[:, k * TC:(k + 1) * TC],
                            start=(k == 0), stop=(k == NJ - 1),
                            skip_group_check=True)),
                        waits=([("ld", ld_w), ("V", v_xn),
                                ("V", pswar)] if k == 0 else []),
                        inc=("Tn" if k == NJ - 1 else None))
                t_hid_fb[fb] = t_hid
                S.emit("scalar",
                       (lambda e, fb=fb, p=p, hbank=hbank: e.activation(
                           fs_s[:, p * 256:p * 256 + 256],
                           hbank[:, 0:256], AF.Sin,
                           bias=vec_s[:, fb:fb + 1], scale=0.1)),
                       waits=[("Tn", t_hid), ("V", pswar)])
                S.emit("scalar",
                       (lambda e, fb=fb, p=p, hbank=hbank: e.activation(
                           fs_c[:, p * 256:p * 256 + 256],
                           hbank[:, 0:256], AF.Sin,
                           bias=vec_c[:, fb:fb + 1], scale=0.1)))
                a_sc = S.emit("scalar",
                              (lambda e, fb=fb, p=p: e.activation(
                                  fs_c[:, p * 256:p * 256 + 256],
                                  fs_c[:, p * 256:p * 256 + 256], AF.Copy,
                                  bias=0.0, scale=P1[:, fb:fb + 1])),
                              inc="A")
                S.emit("vector",
                       (lambda e, fb=fb, p=p: e.scalar_tensor_tensor(
                           fs_s[:, p * 256:p * 256 + 256],
                           fs_s[:, p * 256:p * 256 + 256],
                           P2[:, fb:fb + 1],
                           fs_c[:, p * 256:p * 256 + 256],
                           ALU.mult, ALU.add)),
                       waits=[("A", a_sc)])
                v_ffn_fb[fb] = S.emit(
                    "vector",
                    (lambda e, fb=fb, p=p, hbank=hbank: e.scalar_tensor_tensor(
                        fs_s[:, p * 256:p * 256 + 256],
                        hbank[:, 0:256],
                        b1col_s[:, fb:fb + 1],
                        fs_s[:, p * 256:p * 256 + 256],
                        ALU.add, ALU.add)),
                    inc="V")
                a_relu = S.emit("scalar",
                                (lambda e, fb=fb, p=p: e.activation(
                                    act[:, fb * TC:(fb + 1) * TC],
                                    fs_s[:, p * 256:p * 256 + 256], AF.Relu,
                                    bias=0.0, scale=1.0)),
                                waits=[("V", v_ffn_fb[fb])], inc="A")
            t_hid_prev = [t_hid_fb[NFB - 2], t_hid_fb[NFB - 1]]
            v_ffn_prev = [v_ffn_fb[NFB - 2], v_ffn_fb[NFB - 1]]

            s_asm = delta
            t_w2_J = [0] * NJ
            v_asm_J = [0] * NJ
            for j in range(NJ):
                pj = j % 2
                S.emit("sync",
                       (lambda e, j=j, pj=pj: e.dma_start(
                           out=w2buf[:, pj * 4096:(pj + 1) * 4096],
                           in_=w2X[:, j * 4096:(j + 1) * 4096])),
                       waits=[("Tn", t_w2_J[j - 2] if j >= 2
                               else t_w2_prev[pj])] +
                             ([("lg", lg_prologue)] if (c == 0 and j < 2)
                              else []),
                       inc="ld", amt=16)
                ld_w2 = S.c("ld")
                bank = psF[2 + pj]
                aswar = v_asm_J[j - 2] if j >= 2 else v_asm_prevJ[pj]
                for fb in range(NFB):
                    t_w2_J[j] = S.emit(
                        "tensor",
                        (lambda e, j=j, fb=fb, pj=pj, bank=bank: e.matmul(
                            bank[:, 0:256],
                            w2buf[:, pj * 4096 + fb * 128:
                                  pj * 4096 + (fb + 1) * 128],
                            act[:, fb * TC:(fb + 1) * TC],
                            start=(fb == 0), stop=(fb == NFB - 1),
                            skip_group_check=True)),
                        waits=([("ld", ld_w2), ("A", a_relu),
                                ("V", aswar)] if fb == 0 else []),
                        inc=("Tn" if fb == NFB - 1 else None))
                S.emit("vector",
                       (lambda e, j=j, x_c=x_c, s_asm=s_asm:
                        e.scalar_tensor_tensor(
                            s_asm[:, j * TC:(j + 1) * TC],
                            xn2[:, j * TC:(j + 1) * TC],
                            dskcol_s[:, j:j + 1],
                            x_c[:, j * TC:(j + 1) * TC],
                            ALU.mult, ALU.add)),
                       waits=([("A", a_dA_last),
                               ("lg", lg_store[c - 2] if c >= 2 else 0)]
                              if j == 0 else []))
                S.emit("vector",
                       (lambda e, j=j, s_asm=s_asm: e.tensor_add(
                           s_asm[:, j * TC:(j + 1) * TC],
                           s_asm[:, j * TC:(j + 1) * TC],
                           xn[:, j * TC:(j + 1) * TC])))
                S.emit("vector",
                       (lambda e, j=j, s_asm=s_asm: e.tensor_add(
                           s_asm[:, j * TC:(j + 1) * TC],
                           s_asm[:, j * TC:(j + 1) * TC],
                           y_sb[:, j * TC:(j + 1) * TC])))
                v_asm_J[j] = S.emit(
                    "vector",
                    (lambda e, j=j, s_asm=s_asm, out_c=out_c, bank=bank:
                     e.scalar_tensor_tensor(
                        out_c[:, j * TC:(j + 1) * TC],
                        bank[:, 0:256],
                        b2col_s[:, j:j + 1],
                        s_asm[:, j * TC:(j + 1) * TC],
                        ALU.add, ALU.add)),
                    waits=[("Tn", t_w2_J[j])], inc="V")
            t_w2_prev = [t_w2_J[NJ - 2], t_w2_J[NJ - 1]]
            v_asm_prevJ = [v_asm_J[NJ - 2], v_asm_J[NJ - 1]]
            v_asm[c] = v_asm_J[NJ - 1]
            v_xn_prev = v_xn
            v_xn2_prev = v_xn2

            lg_store[c] = S.emit(
                "gpsimd",
                (lambda e, c=c, out_c=out_c: e.dma_start(
                    out=outT_r[:, :, c * TC:(c + 1) * TC],
                    in_=out_c.rearrange("p (j t) -> p j t", j=NJ))),
                waits=[("V", v_asm[c])], inc="lg", amt=16)

        S.emit("gpsimd", lambda e: None, waits=[("lg", S.c("lg"))])

        @block.sync
        def _(eng):
            S.replay("sync", eng)

        @block.vector
        def _(eng):
            S.replay("vector", eng)

        @block.scalar
        def _(eng):
            S.replay("scalar", eng)

        @block.tensor
        def _(eng):
            S.replay("tensor", eng)

        @block.gpsimd
        def _(eng):
            S.replay("gpsimd", eng)

    return nc


def kernel(**inputs):
    x = np.asarray(inputs["L0_attn_x"], dtype=np.float32)
    gamma = np.asarray(inputs["L0_attn_gamma"], dtype=np.float32)
    beta = np.asarray(inputs["L0_attn_beta"], dtype=np.float32)
    W_delta = np.asarray(inputs["L0_attn_W_delta"], dtype=np.float32)
    b_delta = np.asarray(inputs["L0_attn_b_delta"], dtype=np.float32)
    W_B = np.asarray(inputs["L0_attn_W_B"], dtype=np.float32)
    b_B = np.asarray(inputs["L0_attn_b_B"], dtype=np.float32)
    W_C = np.asarray(inputs["L0_attn_W_C"], dtype=np.float32)
    b_C = np.asarray(inputs["L0_attn_b_C"], dtype=np.float32)
    A_log = np.asarray(inputs["L0_attn_A_log"], dtype=np.float32)
    D_skip = np.asarray(inputs["L0_attn_D_skip"], dtype=np.float32)
    W1 = np.asarray(inputs["L0_ffn_W1"], dtype=np.float32)
    b1 = np.asarray(inputs["L0_ffn_b1"], dtype=np.float32)
    qp = np.asarray(inputs["L0_ffn_quantum_phases"], dtype=np.float32)
    wr = np.asarray(inputs["L0_ffn_quantum_weights_real"], dtype=np.float32)
    wi = np.asarray(inputs["L0_ffn_quantum_weights_imag"], dtype=np.float32)
    W2 = np.asarray(inputs["L0_ffn_W2"], dtype=np.float32)
    b2 = np.asarray(inputs["L0_ffn_b2"], dtype=np.float32)

    def col8(v):
        return np.ascontiguousarray(v.reshape(NJ, 128).T)

    def col32(v):
        return np.ascontiguousarray(v.reshape(NFB, 128).T)

    alogp = np.ascontiguousarray(
        A_log.reshape(NJ, 128, NST).transpose(1, 0, 2).reshape(128, 128))
    wbp = np.ascontiguousarray(
        W_B.reshape(NJ, 128, NST).transpose(1, 0, 2).reshape(128, 128))
    wcp = np.ascontiguousarray(
        W_C.reshape(NJ, 128, NST).transpose(1, 0, 2).reshape(128, 128))
    bcbiasB = np.ascontiguousarray(b_B.reshape(16, 1))
    bcbiasC = np.ascontiguousarray(b_C.reshape(16, 1))
    selcat = np.zeros((16, 16 * 128), np.float32)
    for n in range(16):
        selcat[n, n * 128:(n + 1) * 128] = 1.0

    shared = dict(
        wd=np.ascontiguousarray(W_delta), w1=np.ascontiguousarray(W1),
        w2=np.ascontiguousarray(W2), wbp=wbp, wcp=wcp, alogp=alogp,
        gcol=col8(gamma), bcol=col8(beta), bdcol=col8(b_delta),
        dskcol=col8(D_skip), b2col=col8(b2), b1col=col32(b1),
        qpcol=col32(qp), wrcol=col32(wr), wicol=col32(wi),
        bcbiasB=bcbiasB, bcbiasC=bcbiasC, selcat=selcat,
        eye_in=np.eye(128, dtype=np.float32),
        onescol_in=np.ones((128, 1), np.float32),
        onesrow=np.ones((1, 128), np.float32),
    )
    in_maps = []
    for b in range(8):
        m = dict(shared)
        m["xT"] = np.ascontiguousarray(x[b].T)
        in_maps.append(m)

    nc = build_nc()
    res = run_bass_kernel_spmd(nc, in_maps, core_ids=list(range(8)))
    out = np.empty((8, T, D), np.float32)
    for b in range(8):
        out[b] = res.results[b]["outT"].T
    return out


# version 26
# speedup vs baseline: 1.1776x; 1.0523x over previous
import math
from contextlib import ExitStack

import numpy as np

import concourse.bass as bass
import concourse.mybir as mybir
from concourse.bass_utils import run_bass_kernel_spmd

F32 = mybir.dt.float32
BF16 = mybir.dt.bfloat16
I32 = mybir.dt.int32
AF = mybir.ActivationFunctionType
ALU = mybir.AluOpType

T = 4096
D = 1024
NST = 16
TC = 256
NCH = T // TC
NJ = D // 128
SEG = TC + 2
SCW = NST * SEG
DFF = 4096
NFB = DFF // 128
EPS = 1e-5
TWO_PI_SAFE = 6.2831845


class Sched:

    ENGINES = ("sync", "vector", "scalar", "tensor", "gpsimd")

    def __init__(self, nc, es):
        self.nc = nc
        self.ops = {e: [] for e in self.ENGINES}
        self.sem_h = {}
        self.cnt = {}
        for name in ("ld", "lg", "V", "A", "Tn"):
            self.sem_h[name] = es.enter_context(nc.semaphore(name))
            self.cnt[name] = 0

    tag = "misc"

    def emit(self, eng, fn, waits=(), inc=None, amt=1):
        wl = {}
        for s in waits:
            name, val = s
            if val > 0:
                wl[name] = max(wl.get(name, 0), val)
        self.ops[eng].append((sorted(wl.items()), fn, inc, amt, Sched.tag))
        if inc is not None:
            self.cnt[inc] += amt
        return self.cnt[inc] if inc is not None else None

    def c(self, sem):
        return self.cnt[sem]

    def replay(self, ename, eng):
        need_drain = ename == "vector"
        first = True
        for waits, fn, inc, amt, tag in self.ops[ename]:
            if need_drain and not first:
                eng.drain()
            first = False
            for s, v in waits:
                eng.wait_ge(self.sem_h[s], v)
            with self.nc.named_scope(tag):
                inst = fn(eng)
            if inc is not None:
                if inst is not None:
                    inst.then_inc(self.sem_h[inc], amt)
                else:
                    eng.sem_inc(self.sem_h[inc], amt)


def build_nc():
    nc = bass.Bass("TRN2", target_bir_lowering=False,
                   detect_race_conditions=False)

    def din(name, shape, dt=F32):
        return nc.dram_tensor(name, shape, dt, kind="ExternalInput").ap()

    xT = din("xT", [D, T])
    wd = din("wd", [D, D])
    w1 = din("w1", [D, DFF])
    w2 = din("w2", [DFF, D])
    wbp = din("wbp", [128, 128])
    wcp = din("wcp", [128, 128])
    alogp = din("alogp", [128, 128])
    gcol = din("gcol", [128, NJ])
    bcol = din("bcol", [128, NJ])
    bdcol = din("bdcol", [128, NJ])
    dskcol = din("dskcol", [128, NJ])
    b2col = din("b2col", [128, NJ])
    b1col = din("b1col", [128, NFB])
    qpcol = din("qpcol", [128, NFB])
    wrcol = din("wrcol", [128, NFB])
    wicol = din("wicol", [128, NFB])
    bcbiasB = din("bcbiasB", [16, 1])
    bcbiasC = din("bcbiasC", [16, 1])
    selcat = din("selcat", [16, 16 * 128])
    eye_in = din("eye_in", [128, 128])
    onescol_in = din("onescol_in", [128, 1])
    onesrow = din("onesrow", [1, 128])

    outT = nc.dram_tensor("outT", [D, T], F32, kind="ExternalOutput").ap()
    w1X = nc.dram_tensor("w1X", [128, NFB * 1024], BF16, kind="Internal").ap()
    w2X = nc.dram_tensor("w2X", [128, NFB * 1024], BF16, kind="Internal").ap()

    es = ExitStack()
    with es:
        def sb(name, shape, dt=F32):
            return es.enter_context(nc.sbuf_tensor(name, shape, dt))

        xt = sb("xt", [128, 2 * NJ * TC])
        out_t = sb("out_t", [128, 2 * NJ * TC])
        xn = sb("xn", [128, NJ * TC])
        xnbf = sb("xnbf", [128, NJ * TC], BF16)
        xn2 = sb("xn2", [128, NJ * TC])
        xn2bf = sb("xn2bf", [128, NJ * TC], BF16)
        delta = sb("delta", [128, NJ * TC])
        ubf = sb("ubf", [128, NJ * TC], BF16)
        dAb = sb("dAb", [128, SCW])
        dAb2 = sb("dAb2", [128, SCW])
        wb = sb("wb", [128, SCW], BF16)
        Hb = sb("Hb", [128, SCW], BF16)
        Bexp = sb("Bexp", [128, NST * TC], BF16)
        Cexp = sb("Cexp", [128, NST * TC], BF16)
        stats = sb("stats", [1, 512])
        hstate = sb("hstate", [128, 128], BF16)
        A_pack = sb("A_pack", [128, 128])
        WBbf = sb("WBbf", [128, 128], BF16)
        WCbf = sb("WCbf", [128, 128], BF16)
        Wdbf = sb("Wdbf", [128, NJ * 1024], BF16)
        w1buf = sb("w1buf", [128, 2 * 1024], BF16)
        w2buf = sb("w2buf", [128, 2 * 4096], BF16)
        y_sb = sb("y_sb", [128, NJ * TC], BF16)
        act = sb("act", [128, NFB * TC], BF16)
        fs_s = sb("fs_s", [128, 2 * TC])
        fs_c = sb("fs_c", [128, 2 * TC])
        eye_bf = sb("eye_bf", [128, 128], BF16)
        onescol_bf = sb("onescol_bf", [128, 1], BF16)
        bcmB = sb("bcmB", [16, TC], BF16)
        bcmC = sb("bcmC", [16, TC], BF16)
        cols5 = sb("cols5", [128, 5 * NJ])
        gcol_s = cols5[:, 0 * NJ:1 * NJ]
        bcol_s = cols5[:, 1 * NJ:2 * NJ]
        bdcol_s = cols5[:, 2 * NJ:3 * NJ]
        dskcol_s = cols5[:, 3 * NJ:4 * NJ]
        b2col_s = cols5[:, 4 * NJ:5 * NJ]
        b1col_s = sb("b1col_s", [128, NFB])
        vec_s = sb("vec_s", [128, NFB])
        vec_c = sb("vec_c", [128, NFB])
        P1 = sb("P1", [128, NFB])
        P2 = sb("P2", [128, NFB])
        wr_s = fs_c[:, 3 * NFB:4 * NFB]
        wi_s = fs_c[:, 4 * NFB:5 * NFB]
        qp_s = fs_c[:, 0 * NFB:1 * NFB]
        scr1 = fs_c[:, 1 * NFB:2 * NFB]
        scr2 = fs_c[:, 2 * NFB:3 * NFB]
        bcb2 = sb("bcb2", [16, 2])
        bcbB_s = bcb2[:, 0:1]
        bcbC_s = bcb2[:, 1:2]
        onesrow_s = sb("onesrow_s", [1, 128])
        selcat_bf = sb("selcat_bf", [16, 16 * 128], BF16)
        cst = sb("cst", [128, 4])
        thr_s = fs_c[:, 5 * NFB:5 * NFB + 16]

        def ps(name):
            return es.enter_context(nc.psum_tensor(name, [128, 512], F32))
        psA = ps("psA")
        psB = ps("psB")
        psC = ps("psC")
        psE = ps("psE")
        psF = [ps(f"psF{i}") for i in range(4)]

        S = Sched(nc, es)
        block = es.enter_context(nc.Block())

        xT_r = xT.rearrange("(j p) t -> p j t", p=128)
        outT_r = outT.rearrange("(j p) t -> p j t", p=128)

        def dma(out_ap, in_ap):
            def f(e, o=out_ap, i=in_ap):
                return e.dma_start(out=o, in_=i)
            return f

        for o, i in [
            (A_pack[:, :], alogp), (gcol_s[:, :], gcol), (bcol_s[:, :], bcol),
            (bdcol_s[:, :], bdcol), (dskcol_s[:, :], dskcol),
            (b2col_s[:, :], b2col), (b1col_s[:, :], b1col), (qp_s[:, :], qpcol),
            (wr_s[:, :], wrcol), (bcbB_s[:, :], bcbiasB),
            (bcbC_s[:, :], bcbiasC), (wi_s[:, :], wicol),
            (onesrow_s[:, :], onesrow),
        ]:
            S.emit("sync", dma(o, i), inc="ld", amt=16)
        S.emit("sync", dma(dAb[:16, 0:2048], selcat), inc="ld", amt=16)
        S.emit("sync", dma(dAb[:, 2048:2176], eye_in), inc="ld", amt=16)
        S.emit("sync", dma(dAb[:, 2176:2177], onescol_in), inc="ld", amt=16)
        S.emit("sync", dma(dAb[:, 2432:2560], wbp), inc="ld", amt=16)
        S.emit("sync", dma(dAb[:, 2560:2688], wcp), inc="ld", amt=16)
        ld_small = S.c("ld")
        S.emit("vector", lambda e: e.memset(cst[:, 0:1], EPS))
        S.emit("vector", lambda e: e.memset(cst[:, 1:2], 1.0))
        S.emit("vector", lambda e: e.memset(cst[:, 2:3], 0.0))
        S.emit("vector", lambda e: e.memset(cst[:, 3:4], 1.5707962))
        for jj in range(16):
            b = -(jj + 0.5) if jj < 8 else (jj - 8 + 0.5)
            S.emit("vector", (lambda e, jj=jj, b=b: e.memset(
                thr_s[:, jj:jj + 1], b)))
        S.emit("vector", lambda e: e.tensor_copy(selcat_bf[:, :], dAb[:16, 0:2048]),
               waits=[("ld", ld_small)])
        S.emit("vector", lambda e: e.tensor_copy(eye_bf[:, :], dAb[:, 2048:2176]))
        S.emit("vector", lambda e: e.tensor_copy(onescol_bf[:, :], dAb[:, 2176:2177]))
        S.emit("vector", lambda e: e.tensor_copy(WBbf[:, :], dAb[:, 2432:2560]))
        v_consts = S.emit("vector",
                          lambda e: e.tensor_copy(WCbf[:, :], dAb[:, 2560:2688]),
                          inc="V")

        a_exp = S.emit("scalar",
                       lambda e: e.activation(A_pack[:, :], A_pack[:, :], AF.Exp,
                                              bias=0.0, scale=1.0),
                       waits=[("ld", ld_small)], inc="A")
        S.emit("vector", lambda e: e.tensor_scalar_mul(A_pack[:, :], A_pack[:, :],
                                                       -1.0),
               waits=[("A", a_exp)])
        v_y = S.emit("vector",
                     lambda e: e.tensor_scalar_mul(scr1[:, :], qp_s[:, :],
                                                   1.0 / (2 * math.pi)),
                     inc="V")
        for jj in range(16):
            b = -(jj + 0.5) if jj < 8 else (jj - 8 + 0.5)
            S.emit("scalar",
                   (lambda e, jj=jj: e.activation(
                       fs_s[:, jj * NFB:(jj + 1) * NFB], scr1[:, :], AF.Sign,
                       bias=thr_s[:, jj:jj + 1], scale=1.0)),
                   waits=[("V", v_y)] if jj == 0 else [],
                   inc=("A" if jj == 15 else None))
        a_signs = S.c("A")
        S.emit("vector", lambda e: e.tensor_copy(scr2[:, :], fs_s[:, 0:NFB]),
               waits=[("A", a_signs)])
        for jj in range(1, 16):
            S.emit("vector", (lambda e, jj=jj: e.tensor_add(
                scr2[:, :], scr2[:, :], fs_s[:, jj * NFB:(jj + 1) * NFB])))
        S.emit("vector", lambda e: e.tensor_scalar_mul(scr2[:, :], scr2[:, :],
                                                       0.5))
        v_frac = S.emit("vector",
                        lambda e: e.tensor_sub(scr1[:, :], scr1[:, :], scr2[:, :]),
                        inc="V")
        S.emit("scalar", lambda e: e.activation(scr2[:, :], scr1[:, :], AF.Abs,
                                                bias=0.0, scale=1.0),
               waits=[("V", v_frac)])
        S.emit("scalar", lambda e: e.activation(P1[:, :], scr1[:, :], AF.Sin,
                                                bias=0.0, scale=TWO_PI_SAFE))
        a_sc2 = S.emit("scalar",
                       lambda e: e.activation(P2[:, :], scr2[:, :], AF.Sin,
                                              bias=cst[:, 3:4],
                                              scale=-TWO_PI_SAFE),
                       inc="A")
        S.emit("vector", lambda e: e.tensor_mul(scr1[:, :], P2[:, :], wr_s[:, :]),
               waits=[("A", a_sc2)])
        S.emit("vector", lambda e: e.tensor_mul(scr2[:, :], P1[:, :], wi_s[:, :]))
        S.emit("vector", lambda e: e.tensor_add(scr1[:, :], scr1[:, :], scr2[:, :]))
        S.emit("vector", lambda e: e.tensor_mul(scr1[:, :], scr1[:, :], wi_s[:, :]))
        S.emit("vector", lambda e: e.tensor_scalar_mul(scr1[:, :], scr1[:, :], 0.1))
        S.emit("vector", lambda e: e.tensor_mul(P1[:, :], P1[:, :], scr1[:, :]))
        S.emit("vector", lambda e: e.tensor_mul(P2[:, :], P2[:, :], scr1[:, :]))
        S.emit("vector", lambda e: e.tensor_scalar_mul(vec_s[:, :], b1col_s[:, :],
                                                       0.1))
        S.emit("vector", lambda e: e.tensor_scalar(vec_c[:, :], b1col_s[:, :], 0.1,
                                                   math.pi / 2.0, ALU.mult,
                                                   ALU.add),
               inc="V")

        ld_x = [S.emit("sync",
                       dma(xt[:, 0:NJ * TC].rearrange("p (j t) -> p j t", j=NJ),
                           xT_r[:, :, 0:TC]), inc="ld", amt=16)]

        wd_r = wd.rearrange("(k p) d -> p k d", p=128)
        w1_r = w1.rearrange("(k p) f -> p k f", p=128)
        w2_r = w2.rearrange("(fb p) d -> p fb d", p=128)
        w1X_r = w1X.rearrange("p (fb j c) -> p fb j c", fb=NFB, j=NJ)

        stage_f = [dAb[:, 0:2048], dAb[:, 2048:4096]]
        stage_b = [wb[:, 0:2048], wb[:, 2048:4096]]
        v_cast_prev = {0: v_consts, 1: v_consts}
        lg_out_prev = {0: 0, 1: 0}

        for k in range(NJ):
            p = k % 2
            ldv = S.emit("sync", dma(stage_f[p][:, 0:1024], wd_r[:, k, :]),
                         waits=[("V", v_cast_prev[p])], inc="ld", amt=16)
            v_cast_prev[p] = S.emit(
                "vector",
                (lambda e, k=k, p=p: e.tensor_copy(
                    Wdbf[:, k * 1024:(k + 1) * 1024], stage_f[p][:, 0:1024])),
                waits=[("ld", ldv)], inc="V")
        w2X_r = w2X.rearrange("p (j fb c) -> p j fb c", j=NJ, fb=NFB)
        for fb in range(NFB):
            p = fb % 2
            ldv = S.emit("sync", dma(stage_f[p][:, 0:1024], w2_r[:, fb, :]),
                         waits=[("V", v_cast_prev[p])], inc="ld", amt=16)
            v_cast_prev[p] = S.emit(
                "vector",
                (lambda e, p=p: e.tensor_copy(stage_b[p][:, 0:1024],
                                              stage_f[p][:, 0:1024])),
                waits=[("ld", ldv), ("lg", lg_out_prev[p])], inc="V")
            lg_out_prev[p] = S.emit(
                "gpsimd",
                (lambda e, fb=fb, p=p: e.dma_start(
                    out=w2X_r[:, :, fb, :],
                    in_=stage_b[p][:, 0:1024].rearrange(
                        "p (j c) -> p j c", j=NJ))),
                waits=[("V", v_cast_prev[p])], inc="lg", amt=16)
        for k in range(NJ):
            for h in range(2):
                p = h
                ldv = S.emit("sync",
                             dma(stage_f[p][:, 0:2048],
                                 w1_r[:, k, h * 2048:(h + 1) * 2048]),
                             waits=[("V", v_cast_prev[p])], inc="ld", amt=16)
                v_cast_prev[p] = S.emit(
                    "vector",
                    (lambda e, p=p: e.tensor_copy(stage_b[p][:, 0:2048],
                                                  stage_f[p][:, 0:2048])),
                    waits=[("ld", ldv), ("lg", lg_out_prev[p])], inc="V")
                lg_out_prev[p] = S.emit(
                    "gpsimd",
                    (lambda e, k=k, h=h, p=p: e.dma_start(
                        out=w1X_r[:, h * 16:(h + 1) * 16, k, :],
                        in_=stage_b[p][:, 0:2048].rearrange(
                            "p (fb c) -> p fb c", fb=16))),
                    waits=[("V", v_cast_prev[p])], inc="lg", amt=16)
        lg_prologue = S.c("lg")

        S.emit("vector", lambda e: e.memset(dAb[:, :], 0.0),
               waits=[("lg", lg_prologue)])
        S.emit("vector", lambda e: e.memset(dAb2[:, :], 0.0))
        S.emit("vector", lambda e: e.memset(wb[:, :], 0.0))
        v_zeroed = S.emit("vector", lambda e: e.memset(hstate[:, :], 0.0),
                          inc="V")

        dA_seg2 = [dAb[:].rearrange("p (n s) -> p n s", n=NST),
                   dAb2[:].rearrange("p (n s) -> p n s", n=NST)]
        dA_flat2 = [dAb, dAb2]
        wb_seg = wb[:].rearrange("p (n s) -> p n s", n=NST)
        Hb_seg = Hb[:].rearrange("p (n s) -> p n s", n=NST)
        Bexp_seg = Bexp[:].rearrange("p (n t) -> p n t", n=NST)
        Cexp_seg = Cexp[:].rearrange("p (n t) -> p n t", n=NST)

        st_mu = stats[:, 0:256]
        st_msq = stats[:, 256:512]
        st_tmp = fs_s[0:1, 0:256]
        st_rstd = fs_s[0:1, 0:256]

        v_asm = [0] * (NCH + 1)
        lg_store = [0] * (NCH + 1)
        a_expevac_prev = 0
        t_y_prev_chunk = 0
        v_xn_prev = 0
        v_xn2_prev = 0
        v_bcm_prev = 0
        t_hid_prev = [0, 0]
        t_w2_prev = [0, 0]
        v_ffn_prev = [0, 0, 0, 0]
        v_ffn_prev2 = [0, 0]
        v_asm_prevJ = [0, 0]
        a_yevac_prev = [0, 0]
        v_scan_prev_c = [0, 0]
        a_dA_last = 0

        for c in range(NCH):
            pc = c % 2
            xo = pc * NJ * TC
            x_c = xt[:, xo:xo + NJ * TC]
            out_c = out_t[:, xo:xo + NJ * TC]
            xbf = ubf
            sqbf = Hb[:, 0:NJ * TC]

            if c + 1 < NCH:
                ld_x.append(S.emit(
                    "sync",
                    (lambda e, c=c: e.dma_start(
                        out=xt[:, ((c + 1) % 2) * NJ * TC:
                               (((c + 1) % 2) + 1) * NJ * TC].rearrange(
                            "p (j t) -> p j t", j=NJ),
                        in_=xT_r[:, :, (c + 1) * TC:(c + 2) * TC])),
                    waits=[("V", v_asm[c - 1] if c >= 1 else 0)],
                    inc="ld", amt=16))

            Sched.tag = f"c{c:02d}_ln"
            v_xbf = S.emit("vector",
                           (lambda e, x_c=x_c: e.tensor_copy(xbf[:, :], x_c)),
                           waits=[("ld", ld_x[c]), ("V", v_zeroed)], inc="V")
            a_sq = S.emit("scalar",
                          lambda e: e.activation(sqbf, xbf[:, :], AF.Square,
                                                 bias=0.0, scale=1.0),
                          waits=[("V", v_xbf), ("Tn", t_y_prev_chunk)], inc="A")
            for j in range(NJ):
                S.emit("tensor",
                       (lambda e, j=j: e.matmul(
                           psA[0:1, 0:256], onescol_bf[:, :],
                           xbf[:, j * TC:(j + 1) * TC],
                           start=(j == 0), stop=(j == NJ - 1),
                           skip_group_check=True)),
                       waits=([("V", v_xbf), ("V", v_xn_prev),
                               ("A", a_expevac_prev)] if j == 0 else []))
            t_sums = 0
            for j in range(NJ):
                t_sums = S.emit(
                    "tensor",
                    (lambda e, j=j: e.matmul(
                        psA[0:1, 256:512], onescol_bf[:, :],
                        sqbf[:, j * TC:(j + 1) * TC],
                        start=(j == 0), stop=(j == NJ - 1),
                        skip_group_check=True)),
                    waits=[("A", a_sq)] if j == 0 else [],
                    inc=("Tn" if j == NJ - 1 else None))
            S.emit("vector", lambda e: e.tensor_scalar_mul(st_mu, psA[0:1, 0:256],
                                                           1.0 / D),
                   waits=[("Tn", t_sums)])
            S.emit("vector", lambda e: e.tensor_scalar_mul(st_msq,
                                                           psA[0:1, 256:512],
                                                           1.0 / D))
            S.emit("vector", lambda e: e.tensor_mul(st_tmp, st_mu, st_mu))
            v_var = S.emit("vector",
                           lambda e: e.tensor_sub(st_msq, st_msq, st_tmp),
                           inc="V")
            a_rstd = S.emit("scalar",
                            lambda e: e.activation(st_rstd, st_msq, AF.Sqrt,
                                                   bias=cst[0:1, 0:1], scale=1.0),
                            waits=[("V", v_var)], inc="A")
            S.emit("vector", lambda e: e.reciprocal(st_rstd, st_rstd),
                   waits=[("A", a_rstd)])
            v_beta = S.emit("vector",
                            lambda e: e.scalar_tensor_tensor(
                                st_mu, st_mu, -1.0, st_rstd,
                                ALU.mult, ALU.mult),
                            inc="V")
            S.emit("tensor", lambda e: e.matmul(psA[:, 256:512], onesrow_s[:, :],
                                                st_rstd, start=True, stop=True,
                                                skip_group_check=True),
                   waits=[("V", v_beta), ("A", a_expevac_prev)])
            t_eab = S.emit("tensor",
                           lambda e: e.matmul(psA[:, 0:256], onesrow_s[:, :],
                                              st_mu, start=True, stop=True,
                                              skip_group_check=True),
                           inc="Tn")
            for j in range(NJ):
                S.emit("vector",
                       (lambda e, j=j, x_c=x_c: e.tensor_mul(
                           xn[:, j * TC:(j + 1) * TC],
                           x_c[:, j * TC:(j + 1) * TC], psA[:, 256:512])),
                       waits=[("Tn", t_eab)] if j == 0 else [])
                S.emit("vector",
                       (lambda e, j=j: e.tensor_add(
                           xn[:, j * TC:(j + 1) * TC],
                           xn[:, j * TC:(j + 1) * TC], psA[:, 0:256])))
            v_xn = S.emit("vector",
                          lambda e: e.tensor_copy(xnbf[:, :], xn[:, :]), inc="V")

            Sched.tag = f"c{c:02d}_ln"
            a_sq2 = S.emit("scalar",
                           lambda e: e.activation(sqbf, xnbf[:, :], AF.Square,
                                                  bias=0.0, scale=1.0),
                           waits=[("V", v_xn)], inc="A")
            t_sum2 = 0
            for j in range(NJ):
                t_sum2 = S.emit(
                    "tensor",
                    (lambda e, j=j: e.matmul(
                        psC[0:1, 0:256], onescol_bf[:, :],
                        sqbf[:, j * TC:(j + 1) * TC],
                        start=(j == 0), stop=(j == NJ - 1),
                        skip_group_check=True)),
                    waits=([("A", a_sq2), ("V", v_xn2_prev),
                            ("V", v_bcm_prev)] if j == 0 else []),
                    inc=("Tn" if j == NJ - 1 else None))
            v_m2 = S.emit("vector",
                          lambda e: e.tensor_scalar_mul(st_mu, psC[0:1, 0:256],
                                                        1.0 / D),
                          waits=[("Tn", t_sum2)], inc="V")
            a_rstd2 = S.emit("scalar",
                             lambda e: e.activation(st_msq, st_mu, AF.Sqrt,
                                                    bias=cst[0:1, 0:1], scale=1.0),
                             waits=[("V", v_m2)], inc="A")
            v_rstd2 = S.emit("vector", lambda e: e.reciprocal(st_msq, st_msq),
                             waits=[("A", a_rstd2)], inc="V")
            t_er2 = S.emit("tensor",
                           lambda e: e.matmul(psC[:, 256:512], onesrow_s[:, :],
                                              st_msq, start=True, stop=True,
                                              skip_group_check=True),
                           waits=[("V", v_rstd2)], inc="Tn")
            for j in range(NJ):
                S.emit("vector",
                       (lambda e, j=j: e.scalar_tensor_tensor(
                           xn2[:, j * TC:(j + 1) * TC],
                           xn[:, j * TC:(j + 1) * TC],
                           gcol_s[:, j:j + 1], psC[:, 256:512],
                           ALU.mult, ALU.mult)),
                       waits=[("Tn", t_er2)] if j == 0 else [])
                S.emit("vector",
                       (lambda e, j=j: e.tensor_scalar_add(
                           xn2[:, j * TC:(j + 1) * TC],
                           xn2[:, j * TC:(j + 1) * TC], bcol_s[:, j:j + 1])))
            v_xn2 = S.emit("vector",
                           lambda e: e.tensor_copy(xn2bf[:, :], xn2[:, :]),
                           inc="V")

            Sched.tag = f"c{c:02d}_proj"
            a_sp_J = [0] * NJ
            a_exp_J = [0] * NJ
            for j in range(NJ):
                bank = psB if j % 2 == 0 else psC
                half = ((j // 2) % 2) * 256
                war = [("A", a_exp_J[j - 2])] if j >= 2 else \
                    [("A", a_expevac_prev)]
                t_dj = 0
                for k in range(NJ):
                    t_dj = S.emit(
                        "tensor",
                        (lambda e, j=j, k=k, bank=bank, half=half: e.matmul(
                            bank[:, half:half + 256],
                            Wdbf[:, (k * NJ + j) * 128:(k * NJ + j + 1) * 128],
                            xn2bf[:, k * TC:(k + 1) * TC],
                            start=(k == 0), stop=(k == NJ - 1),
                            skip_group_check=True)),
                        waits=([("V", v_xn2)] + war if k == 0 else []),
                        inc=("Tn" if k == NJ - 1 else None))
                a_exp_J[j] = S.emit(
                    "scalar",
                    (lambda e, j=j, bank=bank, half=half: e.activation(
                        fs_s[:, 0:256], bank[:, half:half + 256], AF.Exp,
                        bias=bdcol_s[:, j:j + 1], scale=1.0)),
                    waits=[("Tn", t_dj)], inc="A")
                a_sp_J[j] = S.emit(
                    "scalar",
                    (lambda e, j=j: e.activation(
                        delta[:, j * TC:(j + 1) * TC], fs_s[:, 0:256], AF.Ln,
                        bias=cst[:, 1:2], scale=1.0)),
                    inc="A")

            Sched.tag = f"c{c:02d}_proj"
            t_bc = 0
            for (w_t, cols) in ((WBbf, 0), (WCbf, 256)):
                for k in range(NJ):
                    t_bc = S.emit(
                        "tensor",
                        (lambda e, w_t=w_t, cols=cols, k=k: e.matmul(
                            psC[0:16, cols:cols + 256],
                            w_t[:, k * 16:(k + 1) * 16],
                            xn2bf[:, k * TC:(k + 1) * TC],
                            start=(k == 0), stop=(k == NJ - 1),
                            skip_group_check=True)),
                        waits=([("V", v_xn2), ("A", a_exp_J[NJ - 1])]
                               if k == 0 else []),
                        inc=("Tn" if (k == NJ - 1 and cols == 256) else None))
            S.emit("vector", lambda e: e.tensor_scalar_add(
                bcmB[:, :], psC[0:16, 0:256], bcbB_s[:, :]),
                   waits=[("Tn", t_bc)])
            v_bcm = S.emit("vector",
                           lambda e: e.tensor_scalar_add(
                               bcmC[:, :], psC[0:16, 256:512],
                               bcbC_s[:, :]),
                           inc="V")
            v_bcm_prev = v_bcm

            Sched.tag = f"c{c:02d}_proj"
            v_u = S.emit("vector",
                         lambda e: e.tensor_mul(ubf[:, :], delta[:, :],
                                                xn2[:, :]),
                         waits=[("A", a_sp_J[NJ - 1]), ("Tn", t_sums)], inc="V")

            Sched.tag = f"c{c:02d}_exps"
            v_wz_prev = v_asm[c - 1] if c >= 1 else 0
            a_piece = []
            for (dst_seg, bcsrc) in ((Bexp_seg, bcmB), (Cexp_seg, bcmC)):
                for n in range(NST):
                    m = len(a_piece)
                    bank = psB if m % 2 == 0 else psC
                    half = ((m // 2) % 2) * 256
                    t_piece = S.emit(
                        "tensor",
                        (lambda e, n=n, bcsrc=bcsrc, bank=bank, half=half:
                         e.matmul(
                            bank[:, half:half + 256],
                            selcat_bf[:, n * 128:(n + 1) * 128],
                            bcsrc[:, :],
                            start=True, stop=True, skip_group_check=True)),
                        waits=(([("V", v_bcm), ("A", a_sp_J[NJ - 1])]
                                if m <= 1 else []) +
                               ([("A", a_piece[m - 2])] if m >= 2 else [])),
                        inc="Tn")
                    a_piece.append(S.emit(
                        "scalar",
                        (lambda e, n=n, dst_seg=dst_seg, bank=bank, half=half:
                         e.activation(dst_seg[:, n, :],
                                      bank[:, half:half + 256],
                                      AF.Copy, bias=0.0, scale=1.0)),
                        waits=[("Tn", t_piece), ("V", v_wz_prev)], inc="A"))
            a_expevac = a_piece[-1]
            a_expevac_prev = a_expevac

            Sched.tag = f"c{c:02d}_scan"
            v_scan_prev = 0
            v_scan_J = [0] * NJ
            t_y_J = 0
            a_yevac_J = [0] * NJ
            for j in range(NJ):
                dwar = (v_scan_J[j - 2] if j >= 2
                        else (v_scan_prev_c[j % 2] if c >= 1 else v_zeroed))
                for n in range(NST):
                    a_dA_last = S.emit(
                        "scalar",
                        (lambda e, j=j, n=n: e.activation(
                            dA_seg2[j % 2][:, n, 2:SEG],
                            delta[:, j * TC:(j + 1) * TC],
                            AF.Exp, bias=0.0,
                            scale=A_pack[:, j * 16 + n:j * 16 + n + 1])),
                        waits=([("V", dwar)] if n == 0 else []),
                        inc=("A" if n == NST - 1 else None))
                S.emit("vector",
                       (lambda e, j=j: e.tensor_copy(
                           wb_seg[:, :, 1:2],
                           hstate[:, j * 16:(j + 1) * 16]
                           .rearrange("p (n o) -> p n o", o=1))),
                       waits=([("A", a_expevac), ("V", v_u)] if j == 0 else []))
                S.emit("vector",
                       (lambda e, j=j: e.tensor_mul(
                           wb_seg[:, :, 2:SEG],
                           ubf[:, j * TC:(j + 1) * TC][:, None, :]
                           .broadcast_to([128, NST, TC]),
                           Bexp_seg[:, :, :])))
                S.emit("vector",
                       (lambda e, j=j: e.tensor_tensor_scan(
                           Hb[:, :], dA_flat2[j % 2][:, :], wb[:, :],
                           cst[:, 2:3], ALU.mult, ALU.add)),
                       waits=[("A", a_dA_last),
                              ("Tn", t_y_J if j > 0 else t_sum2)])
                S.emit("vector",
                       (lambda e, j=j: e.tensor_copy(
                           hstate[:, j * 16:(j + 1) * 16]
                           .rearrange("p (n o) -> p n o", o=1),
                           Hb_seg[:, :, SEG - 1:SEG])))
                v_scan_prev = S.emit(
                    "vector",
                    (lambda e: e.tensor_mul(Hb_seg[:, :, 2:SEG],
                                            Hb_seg[:, :, 2:SEG],
                                            Cexp_seg[:, :, :])),
                    inc="V")
                v_scan_J[j] = v_scan_prev
                ybank = psF[j % 2]
                ywar = a_yevac_J[j - 2] if j >= 2 else a_yevac_prev[j % 2]
                for n in range(NST):
                    t_y_J = S.emit(
                        "tensor",
                        (lambda e, j=j, n=n, ybank=ybank: e.matmul(
                            ybank[:, 0:256],
                            eye_bf[:, :], Hb[:, n * SEG + 2:(n + 1) * SEG],
                            start=(n == 0), stop=(n == NST - 1),
                            skip_group_check=True)),
                        waits=([("V", v_scan_prev), ("A", ywar)]
                               if n == 0 else []),
                        inc=("Tn" if n == NST - 1 else None))
                a_yevac_J[j] = S.emit(
                    "scalar",
                    (lambda e, j=j, ybank=ybank: e.activation(
                        y_sb[:, j * TC:(j + 1) * TC], ybank[:, 0:256],
                        AF.Copy, bias=0.0, scale=1.0)),
                    waits=[("Tn", t_y_J)], inc="A")
            t_y_prev_chunk = t_y_J
            a_yevac_prev = [a_yevac_J[NJ - 2], a_yevac_J[NJ - 1]]
            v_scan_prev_c = [v_scan_J[NJ - 2], v_scan_J[NJ - 1]]

            Sched.tag = f"c{c:02d}_ffn"
            t_hid_fb = [0] * NFB
            v_ffn_fb = [0] * NFB
            v_relu = 0
            hbanks = [psE, psA, psB, psC]
            for fb in range(NFB):
                p = fb % 2
                q4 = fb % 4
                S.emit("sync",
                       (lambda e, fb=fb, p=p: e.dma_start(
                           out=w1buf[:, p * 1024:(p + 1) * 1024],
                           in_=w1X[:, fb * 1024:(fb + 1) * 1024])),
                       waits=[("Tn", t_hid_fb[fb - 2] if fb >= 2
                               else t_hid_prev[p])] +
                             ([("lg", lg_prologue)] if (c == 0 and fb < 2)
                              else []),
                       inc="ld", amt=16)
                ld_w = S.c("ld")
                pswar = v_ffn_fb[fb - 4] if fb >= 4 else v_ffn_prev[q4]
                hbank = hbanks[q4]
                t_hid = 0
                for k in range(NJ):
                    t_hid = S.emit(
                        "tensor",
                        (lambda e, k=k, p=p, hbank=hbank: e.matmul(
                            hbank[:, 0:256],
                            w1buf[:, p * 1024 + k * 128:
                                  p * 1024 + (k + 1) * 128],
                            xnbf[:, k * TC:(k + 1) * TC],
                            start=(k == 0), stop=(k == NJ - 1),
                            skip_group_check=True)),
                        waits=([("ld", ld_w), ("V", v_xn),
                                ("V", pswar)] if k == 0 else []),
                        inc=("Tn" if k == NJ - 1 else None))
                t_hid_fb[fb] = t_hid
                S.emit("scalar",
                       (lambda e, fb=fb, p=p, hbank=hbank: e.activation(
                           fs_s[:, p * 256:p * 256 + 256],
                           hbank[:, 0:256], AF.Sin,
                           bias=vec_s[:, fb:fb + 1], scale=0.1)),
                       waits=[("Tn", t_hid),
                              ("V", v_ffn_fb[fb - 2] if fb >= 2
                               else v_ffn_prev2[p])])
                a_sc = S.emit("scalar",
                              (lambda e, fb=fb, p=p, hbank=hbank: e.activation(
                                  fs_c[:, p * 256:p * 256 + 256],
                                  hbank[:, 0:256], AF.Sin,
                                  bias=vec_c[:, fb:fb + 1], scale=0.1)),
                              inc="A")
                S.emit("vector",
                       (lambda e, fb=fb, p=p: e.tensor_scalar_mul(
                           fs_c[:, p * 256:p * 256 + 256],
                           fs_c[:, p * 256:p * 256 + 256],
                           P1[:, fb:fb + 1])),
                       waits=[("A", a_sc)])
                S.emit("vector",
                       (lambda e, fb=fb, p=p: e.scalar_tensor_tensor(
                           fs_s[:, p * 256:p * 256 + 256],
                           fs_s[:, p * 256:p * 256 + 256],
                           P2[:, fb:fb + 1],
                           fs_c[:, p * 256:p * 256 + 256],
                           ALU.mult, ALU.add)))
                v_ffn_fb[fb] = S.emit(
                    "vector",
                    (lambda e, fb=fb, p=p, hbank=hbank: e.scalar_tensor_tensor(
                        fs_s[:, p * 256:p * 256 + 256],
                        hbank[:, 0:256],
                        b1col_s[:, fb:fb + 1],
                        fs_s[:, p * 256:p * 256 + 256],
                        ALU.add, ALU.add)),
                    inc="V")
                v_relu = S.emit(
                    "vector",
                    (lambda e, fb=fb, p=p: e.tensor_scalar_max(
                        act[:, fb * TC:(fb + 1) * TC],
                        fs_s[:, p * 256:p * 256 + 256], 0.0)),
                    inc="V")
            t_hid_prev = [t_hid_fb[NFB - 2], t_hid_fb[NFB - 1]]
            v_ffn_prev = [v_ffn_fb[NFB - 4], v_ffn_fb[NFB - 3],
                          v_ffn_fb[NFB - 2], v_ffn_fb[NFB - 1]]
            v_ffn_prev2 = [v_ffn_fb[NFB - 2], v_ffn_fb[NFB - 1]]

            Sched.tag = f"c{c:02d}_w2asm"
            s_asm = delta
            t_w2_J = [0] * NJ
            v_asm_J = [0] * NJ
            for j in range(NJ):
                pj = j % 2
                S.emit("sync",
                       (lambda e, j=j, pj=pj: e.dma_start(
                           out=w2buf[:, pj * 4096:(pj + 1) * 4096],
                           in_=w2X[:, j * 4096:(j + 1) * 4096])),
                       waits=[("Tn", t_w2_J[j - 2] if j >= 2
                               else t_w2_prev[pj])] +
                             ([("lg", lg_prologue)] if (c == 0 and j < 2)
                              else []),
                       inc="ld", amt=16)
                ld_w2 = S.c("ld")
                bank = psF[2 + pj]
                aswar = v_asm_J[j - 2] if j >= 2 else v_asm_prevJ[pj]
                for fb in range(NFB):
                    t_w2_J[j] = S.emit(
                        "tensor",
                        (lambda e, j=j, fb=fb, pj=pj, bank=bank: e.matmul(
                            bank[:, 0:256],
                            w2buf[:, pj * 4096 + fb * 128:
                                  pj * 4096 + (fb + 1) * 128],
                            act[:, fb * TC:(fb + 1) * TC],
                            start=(fb == 0), stop=(fb == NFB - 1),
                            skip_group_check=True)),
                        waits=([("ld", ld_w2), ("V", v_relu),
                                ("V", aswar)] if fb == 0 else []),
                        inc=("Tn" if fb == NFB - 1 else None))
                S.emit("vector",
                       (lambda e, j=j, x_c=x_c, s_asm=s_asm:
                        e.scalar_tensor_tensor(
                            s_asm[:, j * TC:(j + 1) * TC],
                            xn2[:, j * TC:(j + 1) * TC],
                            dskcol_s[:, j:j + 1],
                            x_c[:, j * TC:(j + 1) * TC],
                            ALU.mult, ALU.add)),
                       waits=([("A", a_dA_last),
                               ("lg", lg_store[c - 2] if c >= 2 else 0)]
                              if j == 0 else []))
                S.emit("vector",
                       (lambda e, j=j, s_asm=s_asm: e.tensor_add(
                           s_asm[:, j * TC:(j + 1) * TC],
                           s_asm[:, j * TC:(j + 1) * TC],
                           xn[:, j * TC:(j + 1) * TC])))
                S.emit("vector",
                       (lambda e, j=j, s_asm=s_asm: e.tensor_add(
                           s_asm[:, j * TC:(j + 1) * TC],
                           s_asm[:, j * TC:(j + 1) * TC],
                           y_sb[:, j * TC:(j + 1) * TC])))
                v_asm_J[j] = S.emit(
                    "vector",
                    (lambda e, j=j, s_asm=s_asm, out_c=out_c, bank=bank:
                     e.scalar_tensor_tensor(
                        out_c[:, j * TC:(j + 1) * TC],
                        bank[:, 0:256],
                        b2col_s[:, j:j + 1],
                        s_asm[:, j * TC:(j + 1) * TC],
                        ALU.add, ALU.add)),
                    waits=[("Tn", t_w2_J[j])], inc="V")
            t_w2_prev = [t_w2_J[NJ - 2], t_w2_J[NJ - 1]]
            v_asm_prevJ = [v_asm_J[NJ - 2], v_asm_J[NJ - 1]]
            v_asm[c] = v_asm_J[NJ - 1]
            v_xn_prev = v_xn
            v_xn2_prev = v_xn2

            Sched.tag = f"c{c:02d}_store"
            lg_store[c] = S.emit(
                "gpsimd",
                (lambda e, c=c, out_c=out_c: e.dma_start(
                    out=outT_r[:, :, c * TC:(c + 1) * TC],
                    in_=out_c.rearrange("p (j t) -> p j t", j=NJ))),
                waits=[("V", v_asm[c])], inc="lg", amt=16)

        S.emit("gpsimd", lambda e: None, waits=[("lg", S.c("lg"))])

        @block.sync
        def _(eng):
            S.replay("sync", eng)

        @block.vector
        def _(eng):
            S.replay("vector", eng)

        @block.scalar
        def _(eng):
            S.replay("scalar", eng)

        @block.tensor
        def _(eng):
            S.replay("tensor", eng)

        @block.gpsimd
        def _(eng):
            S.replay("gpsimd", eng)

    return nc


def kernel(**inputs):
    x = np.asarray(inputs["L0_attn_x"], dtype=np.float32)
    gamma = np.asarray(inputs["L0_attn_gamma"], dtype=np.float32)
    beta = np.asarray(inputs["L0_attn_beta"], dtype=np.float32)
    W_delta = np.asarray(inputs["L0_attn_W_delta"], dtype=np.float32)
    b_delta = np.asarray(inputs["L0_attn_b_delta"], dtype=np.float32)
    W_B = np.asarray(inputs["L0_attn_W_B"], dtype=np.float32)
    b_B = np.asarray(inputs["L0_attn_b_B"], dtype=np.float32)
    W_C = np.asarray(inputs["L0_attn_W_C"], dtype=np.float32)
    b_C = np.asarray(inputs["L0_attn_b_C"], dtype=np.float32)
    A_log = np.asarray(inputs["L0_attn_A_log"], dtype=np.float32)
    D_skip = np.asarray(inputs["L0_attn_D_skip"], dtype=np.float32)
    W1 = np.asarray(inputs["L0_ffn_W1"], dtype=np.float32)
    b1 = np.asarray(inputs["L0_ffn_b1"], dtype=np.float32)
    qp = np.asarray(inputs["L0_ffn_quantum_phases"], dtype=np.float32)
    wr = np.asarray(inputs["L0_ffn_quantum_weights_real"], dtype=np.float32)
    wi = np.asarray(inputs["L0_ffn_quantum_weights_imag"], dtype=np.float32)
    W2 = np.asarray(inputs["L0_ffn_W2"], dtype=np.float32)
    b2 = np.asarray(inputs["L0_ffn_b2"], dtype=np.float32)

    def col8(v):
        return np.ascontiguousarray(v.reshape(NJ, 128).T)

    def col32(v):
        return np.ascontiguousarray(v.reshape(NFB, 128).T)

    alogp = np.ascontiguousarray(
        A_log.reshape(NJ, 128, NST).transpose(1, 0, 2).reshape(128, 128))
    wbp = np.ascontiguousarray(
        W_B.reshape(NJ, 128, NST).transpose(1, 0, 2).reshape(128, 128))
    wcp = np.ascontiguousarray(
        W_C.reshape(NJ, 128, NST).transpose(1, 0, 2).reshape(128, 128))
    bcbiasB = np.ascontiguousarray(b_B.reshape(16, 1))
    bcbiasC = np.ascontiguousarray(b_C.reshape(16, 1))
    selcat = np.zeros((16, 16 * 128), np.float32)
    for n in range(16):
        selcat[n, n * 128:(n + 1) * 128] = 1.0

    shared = dict(
        wd=np.ascontiguousarray(W_delta), w1=np.ascontiguousarray(W1),
        w2=np.ascontiguousarray(W2), wbp=wbp, wcp=wcp, alogp=alogp,
        gcol=col8(gamma), bcol=col8(beta), bdcol=col8(b_delta),
        dskcol=col8(D_skip), b2col=col8(b2), b1col=col32(b1),
        qpcol=col32(qp), wrcol=col32(wr), wicol=col32(wi),
        bcbiasB=bcbiasB, bcbiasC=bcbiasC, selcat=selcat,
        eye_in=np.eye(128, dtype=np.float32),
        onescol_in=np.ones((128, 1), np.float32),
        onesrow=np.ones((1, 128), np.float32),
    )
    in_maps = []
    for b in range(8):
        m = dict(shared)
        m["xT"] = np.ascontiguousarray(x[b].T)
        in_maps.append(m)

    nc = build_nc()
    res = run_bass_kernel_spmd(nc, in_maps, core_ids=list(range(8)))
    out = np.empty((8, T, D), np.float32)
    for b in range(8):
        out[b] = res.results[b]["outT"].T
    return out
